# version 16
# baseline (speedup 1.0000x reference)
import numpy as np
import ml_dtypes

import concourse.bass as bass
import concourse.mybir as mybir
from concourse.bass_utils import run_bass_kernel_spmd

F32 = mybir.dt.float32
BF16 = mybir.dt.bfloat16
I8 = mybir.dt.int8
AF = mybir.ActivationFunctionType
ALU = mybir.AluOpType
AX = mybir.AxisListType
BF = ml_dtypes.bfloat16

S = 2048
DM = 2048
H = 16
HD = 128
INTER = 8192
NB = 4
NCH = S // 128
NOWN = 8
EPS = 1e-5
ATT_SCALE = HD ** -0.5

_SKIP_EVAC = [False]

C_SUM, C_NM, C_M2E, C_VPE, C_RC, C_RSTD, C_NMR, C_SUM2 = range(8)


class Prog:

    ENGINES = ("sync", "gpsimd", "tensor", "vector", "scalar")
    OWN = {"tensor": "t", "vector": "v", "scalar": "s"}
    DMA = ("dc", "dxa", "dxb", "dx2", "dout", "dwq", "dwk", "dwo", "dgu", "dwd", "dpc")

    def __init__(self, sems):
        self.sems = sems
        self.ctr = {k: 0 for k in sems}
        self.streams = {e: [] for e in self.ENGINES}

    def emit(self, engine, fn, waits=(), inc=None, amt=None):
        w = []
        for sname, v in waits:
            if v is None or v <= 0:
                continue
            assert v <= self.ctr[sname], (engine, sname, v, self.ctr[sname])
            if sname in self.DMA:
                assert v == self.ctr[sname], ("partial DMA wait", engine, sname,
                                              v, self.ctr[sname])
            w.append((sname, v))
        post = None
        incs = ()
        if inc is not None:
            if amt is None:
                amt = 16 if inc in self.DMA else 1
            self.ctr[inc] += amt
            post = self.ctr[inc]
            incs = ((inc, amt),)
        self.streams[engine].append((tuple(w), fn, incs))
        return post

    def replay(self, engine, eng):
        last = {}
        for waits, fn, incs in self.streams[engine]:
            for sname, v in waits:
                if last.get(sname, 0) >= v:
                    continue
                eng.wait_ge(self.sems[sname], v)
                last[sname] = v
            inst = fn(eng)
            for sname, amt in incs:
                inst.then_inc(self.sems[sname], amt)


def build_program(limit=None, taps=(), p3_groups=8, p3_pipe=True):
    nc = bass.Bass("TRN2", target_bir_lowering=False,
                   detect_race_conditions=False)

    x_full = nc.dram_tensor("x_full", [S, DM], F32, kind="ExternalInput").ap()
    x_own = nc.dram_tensor("x_own", [NOWN * 128, DM], F32, kind="ExternalInput").ap()
    wq_t = nc.dram_tensor("wq_t", [H, 128, 16, 128], F32, kind="ExternalInput").ap()
    wk_t = nc.dram_tensor("wk_t", [8, 128, 4096], F32, kind="ExternalInput").ap()
    wv_t = nc.dram_tensor("wv_t", [8, 128, 4096], F32, kind="ExternalInput").ap()
    wo_t = nc.dram_tensor("wo_t", [4, 128, 16, 512], F32, kind="ExternalInput").ap()
    wgu_t = nc.dram_tensor("wgu_t", [64, 128, 4096], I8, kind="ExternalInput").ap()
    wd_t = nc.dram_tensor("wd_t", [4, 8, 128, 4096], I8, kind="ExternalInput").ap()
    sg_d = nc.dram_tensor("sg", [128, 64], F32, kind="ExternalInput").ap()
    su_d = nc.dram_tensor("su", [128, 64], F32, kind="ExternalInput").ap()
    sdb_d = nc.dram_tensor("sdb", [128, DM], F32, kind="ExternalInput").ap()
    cosk_d = nc.dram_tensor("cosk", [128, S], BF16, kind="ExternalInput").ap()
    sink_d = nc.dram_tensor("sink2", [128, S], BF16, kind="ExternalInput").ap()
    cosq_d = nc.dram_tensor("cosq", [128, 1024], BF16, kind="ExternalInput").ap()
    sinq_d = nc.dram_tensor("sinq2", [128, 1024], BF16, kind="ExternalInput").ap()
    mask_d = nc.dram_tensor("maskq", [128, NOWN * 256], F32, kind="ExternalInput").ap()
    idb_d = nc.dram_tensor("identb", [128, 128], BF16, kind="ExternalInput").ap()
    idf_d = nc.dram_tensor("identf", [128, 128], F32, kind="ExternalInput").ap()
    ones_d = nc.dram_tensor("ones", [1, 128], F32, kind="ExternalInput").ap()
    y_d = nc.dram_tensor("y", [NOWN * 128, DM], F32, kind="ExternalOutput").ap()
    wgu_bf = nc.dram_tensor("wgu_bf", [64, 128, 4096], BF16, kind="Internal").ap()
    wd_bf = nc.dram_tensor("wd_bf", [4, 8, 128, 4096], BF16, kind="Internal").ap()

    SBASE = 16512
    A0, B0, C0, D0, E0 = (SBASE, SBASE + 65536, SBASE + 98304,
                          SBASE + 131072, SBASE + 196608)

    def at(name, shape, dtype, off):
        return nc.alloc_sbuf_tensor_at(name, shape, dtype, offset=off)

    XT1 = at("XT1", [128, 16 * S], BF16, A0)
    o_full = at("o_full", [128, NOWN * DM], F32, A0)
    fuse = at("fuse", [128, 64 * 512], BF16, A0)

    XTQ = at("XTQ", [128, 16 * 1024], BF16, B0)
    mask_sb = at("mask_sb", [128, NOWN * 256], F32, B0)
    cosk = at("cosk_sb", [128, S], BF16, B0 + 8192)
    sink2 = at("sink_sb", [128, S], BF16, B0 + 12288)
    wk2 = at("wk2", [128, 2 * 16 * 128], BF16, B0 + 16384)
    wv2 = at("wv2", [128, 16 * 256], BF16, B0 + 24576)
    XT3 = at("XT3", [128, 16 * 1024], BF16, B0)

    qall = at("qall", [128, H * 1024], BF16, C0)
    wo_sb = [at(f"wo{i}", [128, 16 * 512], BF16, C0 + i * 16384) for i in (0, 1)]
    wgu_sb = [at(f"wgu{i}", [128, 2 * 16 * 128], BF16, C0 + i * 8192) for i in (0, 1)]
    wd_sb = [at(f"wd{i}", [128, 8 * 512], BF16, C0 + 16384 + i * 8192) for i in (0, 1)]

    xstage = at("xstage", [128, 2 * DM], F32, D0)
    xn = at("xn", [128, 2 * DM], BF16, D0 + 16384)
    wq_sb = [at(f"wq{i}", [128, 16 * 128], BF16, D0 + 24576 + i * 4096) for i in (0, 1)]
    v2_sb = at("v2", [128, 16 * 256], BF16, D0)
    v2_sb2 = at("v2b", [128, 16 * 256], BF16, D0 + 57344)
    k_h = at("k_h", [128, S], BF16, D0 + 8192)
    PT_sb = at("PT", [128, 2 * 1024], BF16, D0 + 12288)
    qsb = at("qsb", [128, 512], BF16, D0 + 32768)
    ksb = at("ksb", [128, 512], BF16, D0 + 56320)
    rope_tmp = at("rope_tmp", [128, 512], BF16, D0 + 53248)
    AVALL = at("AVALL", [128, H * 1024], BF16, D0 + 20480)
    x_res = at("x_res", [128, NOWN * DM], F32, D0)

    p_row = at("p_row", [128, S], BF16, E0)
    p_row2 = at("p_row2", [128, S], BF16, E0 + 4096)
    cosq = at("cosq_sb", [128, 1024], BF16, E0 + 4096)
    sinq2 = at("sinq_sb", [128, 1024], BF16, E0 + 6144)
    xstage2 = at("xstage2", [128, DM], F32, E0)
    y_buf = at("y_buf", [128, DM], F32, E0)
    bc_sb = at("bc_sb", [128, 512], F32, E0 + 8192)
    sdb = at("sdb_sb", [128, DM], BF16, E0 + 8192)
    identb = at("identb_sb", [128, 128], BF16, E0 + 12288)
    identf = at("identf_sb", [128, 128], F32, E0 + 12544)
    ones_sb = at("ones_sb", [1, 128], F32, E0 + 13056)
    sg_sb = at("sg_sb", [128, 64], F32, E0 + 13568)
    su_sb = at("su_sb", [128, 64], F32, E0 + 13824)
    stats = at("stats", [128, 16], F32, E0 + 14080)
    sums8 = at("sums8", [128, 8], F32, E0 + 14144)
    rcp8 = at("rcp8", [128, 8], F32, E0 + 14176)
    nmcol = at("nmcol", [128, 1], F32, E0 + 14208)
    negshift = at("negshift", [128, 1], F32, E0 + 14240)
    rrow = at("rrow", [1, 512], F32, D0 + 54272)
    y_buf2 = at("y_buf2", [128, DM], F32, C0)

    psA = nc.alloc_psum_tensor("psA", [128, 2048], F32)
    psB = nc.alloc_psum_tensor("psB", [128, 512], F32)
    psC = nc.alloc_psum_tensor("psC", [128, 2048], BF16)
    psD = nc.alloc_psum_tensor("psD", [128, 512], F32)

    from contextlib import ExitStack
    ctx = ExitStack()
    sems = {}
    for sname in Prog.DMA + ("t", "v", "s"):
        sems[sname] = ctx.enter_context(nc.semaphore("sem_" + sname))
    P = Prog(sems)

    def _finish():
        for tname in taps:
            th = tapreg[tname]
            shape = list(th.shape)
            td = nc.dram_tensor("tap_" + tname, shape, th.dtype,
                                kind="ExternalOutput").ap()
            P.emit("sync", lambda e, th=th, td=td: e.dma_start(
                out=td, in_=th.ap()),
                waits=[("t", P.ctr["t"]), ("v", P.ctr["v"]), ("s", P.ctr["s"])],
                inc="dout")
        for eng, semlist in (("sync", ("dc", "dxa", "dxb", "dx2", "dout")),
                             ("gpsimd", ("dwq", "dwk", "dwo", "dgu", "dwd"))):
            for sname in semlist:
                v = P.ctr[sname]
                if v:
                    P.emit(eng, lambda e, s=sname, v=v: e.wait_ge(sems[s], v))
        with nc.Block() as block:
            block.sync(lambda e: P.replay("sync", e))
            block.gpsimd(lambda e: P.replay("gpsimd", e))
            block.tensor(lambda e: P.replay("tensor", e))
            block.vector(lambda e: P.replay("vector", e))
            block.scalar(lambda e: P.replay("scalar", e))
        ctx.close()
        return nc


    XT1r = XT1.ap().rearrange("p (j t) -> p j t", j=16)
    XTQr = XTQ.ap().rearrange("p (j t) -> p j t", j=16)
    XT3r = XT3.ap().rearrange("p (j t) -> p j t", j=16)
    psCr = psC.ap().rearrange("p (n t) -> p n t", n=16)

    def ln_stats(src_ap, sq_out_ap, par, first_waits):
        o = 8 * par
        st = stats.ap()
        v0 = P.emit("vector", lambda e, a=src_ap, o=o: e.reduce_sum(
            st[:, o + C_SUM:o + C_SUM + 1], a, axis=AX.X), waits=first_waits,
            inc="v")
        sq = P.emit("scalar", lambda e, a=src_ap, b=sq_out_ap, o=o: e.activation(
            b, a, AF.Square, bias=0.0, scale=1.0,
            accum_out=st[:, o + C_SUM2:o + C_SUM2 + 1]),
            waits=first_waits, inc="s")
        v1 = P.emit("vector", lambda e, o=o: e.tensor_scalar(
            st[:, o + C_NM:o + C_NM + 1], st[:, o + C_SUM:o + C_SUM + 1],
            -1.0 / DM, None, op0=ALU.mult), waits=[("v", v0)], inc="v")
        v2 = P.emit("vector", lambda e, o=o: e.tensor_scalar(
            st[:, o + C_M2E:o + C_M2E + 1], st[:, o + C_NM:o + C_NM + 1],
            st[:, o + C_NM:o + C_NM + 1], EPS, op0=ALU.mult, op1=ALU.subtract),
            waits=[("v", v1)], inc="v")
        v3 = P.emit("vector", lambda e, o=o: e.tensor_scalar(
            st[:, o + C_VPE:o + C_VPE + 1], st[:, o + C_SUM2:o + C_SUM2 + 1],
            1.0 / DM, st[:, o + C_M2E:o + C_M2E + 1],
            op0=ALU.mult, op1=ALU.subtract),
            waits=[("s", sq), ("v", v2)], inc="v")
        vv = P.emit("vector", lambda e, o=o: e.reciprocal(
            st[:, o + C_RC:o + C_RC + 1], st[:, o + C_VPE:o + C_VPE + 1]),
            waits=[("v", v3)], inc="v")
        ss = P.emit("scalar", lambda e, o=o: e.activation(
            st[:, o + C_RSTD:o + C_RSTD + 1], st[:, o + C_RC:o + C_RC + 1],
            AF.Sqrt), waits=[("v", vv)], inc="s")
        vd = P.emit("vector", lambda e, o=o: e.tensor_mul(
            st[:, o + C_NMR:o + C_NMR + 1], st[:, o + C_NM:o + C_NM + 1],
            st[:, o + C_RSTD:o + C_RSTD + 1]), waits=[("s", ss)], inc="v")
        return o, ss, vd

    psC_evac = {0: 0, 1: 0}

    def transpose_batch(src_tiles, dst_ap, half, t_waits=(), evac_waits=(),
                        identity=None):
        idt = identity if identity is not None else identb
        n = len(src_tiles)
        tw = list(t_waits)
        if psC_evac[half]:
            tw.append(("v", psC_evac[half]))
        tt = None
        for k, sap in enumerate(src_tiles):
            tt = P.emit("tensor", lambda e, a=sap, k=k, half=half: e.transpose(
                psC.ap()[:, half * 1024 + k * 128: half * 1024 + (k + 1) * 128],
                a, idt.ap()[:, :]), waits=tw if k == 0 else (),
                inc="t" if k == n - 1 else None)
        if _SKIP_EVAC[0]:
            ve = P.emit("vector", lambda e: e.nop(),
                        waits=[("t", tt)] + list(evac_waits), inc="v")
            psC_evac[half] = ve
            return ve
        src = psCr[:, half * 8: half * 8 + n, :]
        ve = P.emit("vector", lambda e, s=src, d=dst_ap: e.tensor_copy(d, s),
                    waits=[("t", tt)] + list(evac_waits), inc="v")
        psC_evac[half] = ve
        return ve

    tapreg = {"XT1": XT1, "XTQ": XTQ, "stats": stats, "qall": qall,
              "k_h": k_h, "v2": v2_sb, "AVALL": AVALL, "o_full": o_full,
              "x_res": x_res, "fuse": fuse, "xn": xn, "p_row": p_row,
              "sums8": sums8, "rcp8": rcp8, "mask_sb": mask_sb, "cosk": cosk,
              "xstage": xstage, "y_buf": y_buf, "XT3": XT3}

    def fence():
        pass

    P.emit("vector", lambda e: e.memset(negshift.ap(), -8.0), inc="v")
    for cc in range(8):
        P.emit("gpsimd", lambda e, cc=cc: e.dma_start(
            out=wgu_bf[cc * 8:(cc + 1) * 8], in_=wgu_t[cc * 8:(cc + 1) * 8]),
            inc="dpc")
    for hg in range(4):
        P.emit("gpsimd", lambda e, hg=hg: e.dma_start(
            out=wd_bf[hg], in_=wd_t[hg]), inc="dpc")
    dpc_done = P.ctr["dpc"]
    P.emit("sync", lambda e: e.dma_start(out=identb.ap(), in_=idb_d), inc="dc")
    P.emit("sync", lambda e: e.dma_start(out=identf.ap(), in_=idf_d), inc="dc")
    P.emit("sync", lambda e: e.dma_start(out=ones_sb.ap(), in_=ones_d), inc="dc")
    P.emit("sync", lambda e: e.dma_start(out=cosq.ap(), in_=cosq_d), inc="dc")
    P.emit("sync", lambda e: e.dma_start(out=sinq2.ap(), in_=sinq_d), inc="dc")
    P.emit("sync", lambda e: e.dma_start(out=sg_sb.ap(), in_=sg_d), inc="dc")
    dx_const = P.emit("sync", lambda e: e.dma_start(out=su_sb.ap(), in_=su_d),
                      inc="dc")

    norm_done = {}
    trans_t = {}

    def ln_chunk(ci, kind, c):
        par = ci % 2
        src_dram = (x_full if kind == "full" else x_own)[c * 128:(c + 1) * 128, :]
        xs = xstage.ap()[:, par * DM:(par + 1) * DM]
        xnp = xn.ap()[:, par * DM:(par + 1) * DM]
        dsem = "dxa" if par == 0 else "dxb"
        dxc = P.emit("sync", lambda e, a=xs, b=src_dram: e.dma_start(out=a, in_=b),
                     waits=[("s", norm_done.get(ci - 2, 0))], inc=dsem)
        fw = [(dsem, dxc), ("t", trans_t.get(ci - 2, 0))]
        o, ss, vd = ln_stats(xs, xnp, par, fw)
        nd = P.emit("scalar", lambda e, a=xnp, b=xs, o=o: e.activation(
            a, b, AF.Identity,
            bias=stats.ap()[:, o + C_NMR:o + C_NMR + 1],
            scale=stats.ap()[:, o + C_RSTD:o + C_RSTD + 1]),
            waits=[("v", vd)], inc="s")
        norm_done[ci] = nd
        for bt in range(4):
            tiles = [xnp[:, (bt * 4 + k) * 128:(bt * 4 + k + 1) * 128]
                     for k in range(4)]
            r = XT1r if kind == "full" else XTQr
            dst = r[:, bt * 4:bt * 4 + 4, c * 128:(c + 1) * 128]
            transpose_batch(tiles, dst, half=bt % 2, t_waits=[("s", nd)])
        trans_t[ci] = P.ctr["t"]

    for c in range(NOWN):
        ln_chunk(c, "own", c)
    xtq_v_done = P.ctr["v"]

    q_mm_done = {}
    qsb_free = 0
    psQ_free = {0: 0, 1: 0}

    def q_head(h):
        global_unused = None
        wbuf = wq_sb[h % 2]
        w = [("t", q_mm_done.get(h - 2, 0)), ("s", norm_done[NOWN - 1])]
        dwc = P.emit("gpsimd", lambda e, a=wbuf, b=wq_t[h]: e.dma_start(
            out=a.ap(), in_=b), waits=w, inc="dwq")
        nonlocal_state = None
        for G in range(2):
            ps = (psB if G == 0 else psD).ap()[:, 0:512]
            for j in range(16):
                w = [("dwq", dwc), ("s", psQ_free[G]), ("v", xtq_v_done)]
                P.emit("tensor", lambda e, ps=ps, a=wbuf, j=j, G=G: e.matmul(
                    ps, a.ap()[:, j * 128:(j + 1) * 128],
                    XTQr[:, j, G * 512:(G + 1) * 512],
                    start=(j == 0), stop=(j == 15)),
                    waits=w if j == 0 else (), inc="t" if j == 15 else None)
            tq = P.ctr["t"]
            ev = P.emit("scalar", lambda e, ps=ps: e.activation(
                qsb.ap(), ps, AF.Copy), waits=[("t", tq), ("v", q_head.qsb_free)],
                inc="s")
            psQ_free[G] = ev
            qa = qall.ap()[:, h * 1024 + G * 512: h * 1024 + (G + 1) * 512]
            cq = cosq.ap()[:, G * 512:(G + 1) * 512]
            sq2 = sinq2.ap()[:, G * 512:(G + 1) * 512]
            P.emit("vector", lambda e, sq2=sq2: e.tensor_mul(
                rope_tmp.ap()[0:64, :], qsb.ap()[64:128, :], sq2[64:128, :]),
                waits=[("s", ev), ("dc", dx_const)])
            P.emit("vector", lambda e, sq2=sq2: e.tensor_mul(
                rope_tmp.ap()[64:128, :], qsb.ap()[0:64, :], sq2[0:64, :]))
            q_head.qsb_free = P.emit("vector", lambda e, qa=qa, cq=cq: e.tensor_mul(
                qa, qsb.ap(), cq), inc="v")
            P.emit("vector", lambda e, qa=qa: e.tensor_add(
                qa, qa, rope_tmp.ap()), waits=[("v", q_head.qsb_free)], inc="v")
        q_mm_done[h] = P.ctr["t"]
    q_head.qsb_free = 0

    for h in range(H):
        ln_chunk(NOWN + h, "full", h)
        q_head(h)
    ln1_t_done = P.ctr["t"]
    ln1_v_done = P.ctr["v"]
    ln1_s_done = P.ctr["s"]
    if limit == 1:
        return _finish()
    fence()
    p2_t_done = P.ctr["t"]
    p2_s_done = P.ctr["s"]
    p2_v_done = P.ctr["v"]
    if limit == 2:
        return _finish()
    w = [("t", p2_t_done)]
    P.emit("sync", lambda e: e.dma_start(out=mask_sb.ap(), in_=mask_d), waits=w,
           inc="dc")
    P.emit("sync", lambda e: e.dma_start(out=cosk.ap(), in_=cosk_d), inc="dc")
    dx_k = P.emit("sync", lambda e: e.dma_start(out=sink2.ap(), in_=sink_d),
                  inc="dc")
    mask_r = mask_sb.ap().rearrange("p (i c) -> p i c", i=NOWN)

    kv_buf_free = p2_t_done
    v2_free = 0
    k_h_free = 0
    psA_free_s = p2_s_done
    ksb_free = 0
    exp_prev = 0

    class att:
        av_read = 0
        pt_free = {0: 0, 1: 0}
        pt_slot_done = {0: 0, 1: 0}
        psD_free_v = 0
        sums_free = {k: 0 for k in range(8)}

    def vbuf(g2):
        return v2_sb if g2 % 2 == 0 else v2_sb2

    def emit_kv_dma(g2, wk_free_t, v2_free_t):
        P.emit("gpsimd", lambda e, g2=g2: e.dma_start(
            out=wk2.ap(), in_=wk_t[g2]),
            waits=[("t", wk_free_t)], inc="dwk")
        return P.emit("gpsimd", lambda e, g2=g2: e.dma_start(
            out=wv2.ap(), in_=wv_t[g2]),
            waits=[("t", v2_free_t)], inc="dwk")

    def emit_vprod_tc(g2, tc, dwk_val):
        ps = psD.ap()[:, 0:256]
        for j in range(16):
            w = [("dwk", dwk_val), ("v", max(att.vprod_evac, ln1_v_done))]
            P.emit("tensor", lambda e, ps=ps, j=j, tc=tc: e.matmul(
                ps, XT1r[:, j, tc * 128:(tc + 1) * 128],
                wv2.ap()[:, j * 256:(j + 1) * 256],
                start=(j == 0), stop=(j == 15)),
                waits=w if j == 0 else (), inc="t" if j == 15 else None)
        tvp = P.ctr["t"]
        att.vprod_evac = P.emit("vector", lambda e, ps=ps, tc=tc, g2=g2:
            e.tensor_copy(
                vbuf(g2).ap()[:, tc * 256:(tc + 1) * 256], ps),
            waits=[("t", tvp)], inc="v")

    att.vprod_evac = p2_v_done
    dwk_by_group = {0: None}
    vprod_done_by_group = {}
    v2_free_par = {0: 0, 1: 0}

    for g2 in range(p3_groups):
        if g2 == 0:
            dwk_by_group[0] = emit_kv_dma(0, kv_buf_free, 0)
            for tc in range(NCH):
                emit_vprod_tc(0, tc, dwk_by_group[0])
            vprod_done_by_group[0] = P.ctr["v"]
        vprod_done_s = vprod_done_by_group[g2]

        for hh in range(2):
            h = 2 * g2 + hh
            krope_g = {}
            for g in range(4):
                ps = psA.ap()[:, g * 512:(g + 1) * 512]
                for j in range(16):
                    w = [("dwk", dwk_by_group[g2]), ("s", psA_free_s),
                         ("v", ln1_v_done), ("s", ln1_s_done)]
                    P.emit("tensor", lambda e, ps=ps, j=j, hh=hh, g=g: e.matmul(
                        ps,
                        wk2.ap()[:, hh * 2048 + j * 128: hh * 2048 + (j + 1) * 128],
                        XT1r[:, j, g * 512:(g + 1) * 512],
                        start=(j == 0), stop=(j == 15)),
                        waits=w if j == 0 else (), inc="t" if j == 15 else None)
                tk = P.ctr["t"]
                ev = P.emit("scalar", lambda e, ps=ps: e.activation(
                    ksb.ap(), ps, AF.Copy),
                    waits=[("t", tk), ("v", ksb_free)], inc="s")
                kh = k_h.ap()[:, g * 512:(g + 1) * 512]
                ck = cosk.ap()[:, g * 512:(g + 1) * 512]
                sk = sink2.ap()[:, g * 512:(g + 1) * 512]
                P.emit("vector", lambda e, sk=sk: e.tensor_mul(
                    rope_tmp.ap()[0:64, :], ksb.ap()[64:128, :], sk[64:128, :]),
                    waits=[("s", ev), ("dc", dx_k), ("t", k_h_free)])
                P.emit("vector", lambda e, sk=sk: e.tensor_mul(
                    rope_tmp.ap()[64:128, :], ksb.ap()[0:64, :], sk[0:64, :]))
                ksb_free = P.emit("vector", lambda e, kh=kh, ck=ck: e.tensor_mul(
                    kh, ksb.ap(), ck), inc="v")
                krope_g[g] = P.emit("vector", lambda e, kh=kh: e.tensor_add(
                    kh, kh, rope_tmp.ap()), waits=[("v", ksb_free)], inc="v")
            krope_v = P.ctr["v"]
            psA_scores_wait = P.ctr["s"]
            if hh == 1 and g2 + 1 < p3_groups:
                dwk_by_group[g2 + 1] = emit_kv_dma(
                    g2 + 1, P.ctr["t"], v2_free_par[(g2 + 1) % 2])

            exp_done = {}
            evac_done = {}
            sc_done = {}

            def emit_pt_av(i):
                nt = 2 * i + 2
                G = i // 4
                ed = exp_done[i]
                pr = (p_row if i % 2 == 0 else p_row2).ap()
                batches = [list(range(cb * 8, min(cb * 8 + 8, nt)))
                           for cb in range((nt + 7) // 8)]
                ves = []
                for cb, cs in enumerate(batches):
                    half = cb % 2
                    tiles = [pr[:, c * 128:(c + 1) * 128] for c in cs]
                    dst = PT_sb.ap()[:, half * 1024: half * 1024 + len(cs) * 128]
                    ves.append(transpose_batch(
                        tiles, dst, half=half, t_waits=[("v", sc_done[i])],
                        evac_waits=[("t", att.pt_free[half])]))
                for cb, cs in enumerate(batches):
                    half = cb % 2
                    for c in cs:
                        w = [("v", max(ves[cb], vprod_done_s)),
                             ("v", att.av_read)]
                        P.emit("tensor",
                               lambda e, c=c, cb=cb, i=i, half=half, nt=nt,
                               hh=hh, g2=g2: e.matmul(
                                   psB.ap()[:, (i % 4) * 128:(i % 4 + 1) * 128],
                                   vbuf(g2).ap()[:, c * 256 + hh * 128:
                                                 c * 256 + (hh + 1) * 128],
                                   PT_sb.ap()[:, half * 1024 + (c - cb * 8) * 128:
                                              half * 1024 + (c - cb * 8 + 1) * 128],
                                   start=(i % 4 == 0 and c == 0),
                                   stop=(i % 4 == 3 and c == nt - 1),
                                   skip_group_check=True),
                               waits=w if c == cs[0] else (),
                               inc="t" if c == cs[-1] else None)
                    att.pt_free[half] = P.ctr["t"]
                att.pt_slot_done[i % 2] = P.ctr["t"]
                if i % 4 == 3:
                    av_t = P.ctr["t"]
                    att.av_read = P.emit("vector", lambda e, G=G, h=h:
                        e.tensor_copy(
                            AVALL.ap()[:, h * 1024 + G * 512:
                                       h * 1024 + (G + 1) * 512],
                            psB.ap()[:, 0:512]),
                        waits=[("t", av_t)], inc="v")

            for i in range(NOWN):
                W = (2 * i + 2) * 128
                ng = (W + 511) // 512
                for g in range(ng):
                    n = min(512, W - g * 512)
                    w = [("v", krope_g[ng - 1]),
                         ("s", evac_done.get(i - 1, psA_scores_wait))]
                    P.emit("tensor", lambda e, g=g, n=n, h=h, i=i: e.matmul(
                        psA.ap()[:, g * 512:g * 512 + n],
                        qall.ap()[:, h * 1024 + i * 128: h * 1024 + (i + 1) * 128],
                        k_h.ap()[:, g * 512:g * 512 + n],
                        start=True, stop=True),
                        waits=w if g == 0 else (), inc="t" if g == ng - 1 else None)
                tsc = P.ctr["t"]
                if p3_pipe and i > 0:
                    emit_pt_av(i - 1)
                vmask = P.emit("vector", lambda e, W=W, i=i: e.tensor_add(
                    psA.ap()[:, W - 256:W], psA.ap()[:, W - 256:W],
                    mask_r[:, i, :]), waits=[("t", tsc)], inc="v")
                pr = (p_row if i % 2 == 0 else p_row2).ap()
                ed = P.emit("scalar", lambda e, W=W, i=i, pr=pr:
                    e.activation(
                        pr[:, 0:W], psA.ap()[:, 0:W], AF.Exp,
                        bias=negshift.ap()[:, 0:1], scale=1.0,
                        accum_out=sums8.ap()[:, i % 8:i % 8 + 1]),
                    waits=[("v", vmask), ("t", att.pt_slot_done[i % 2]),
                           ("v", att.sums_free[i % 8])], inc="s")
                exp_done[i] = ed
                evac_done[i] = ed
                exp_prev = ed
                rc = P.emit("vector", lambda e, i=i: e.reciprocal(
                    rcp8.ap()[:, i % 8:i % 8 + 1],
                    sums8.ap()[:, i % 8:i % 8 + 1]), waits=[("s", ed)], inc="v")
                att.sums_free[i % 8] = rc
                sc_done[i] = P.emit("vector", lambda e, i=i, pr=pr, W=W:
                    e.tensor_scalar(
                        pr[:, 0:W], pr[:, 0:W],
                        rcp8.ap()[:, i % 8:i % 8 + 1], None, op0=ALU.mult),
                    waits=[("v", rc)], inc="v")
                if hh == 1 and g2 + 1 < p3_groups:
                    emit_vprod_tc(g2 + 1, 2 * i, dwk_by_group[g2 + 1])
                    emit_vprod_tc(g2 + 1, 2 * i + 1, dwk_by_group[g2 + 1])
                if not p3_pipe:
                    emit_pt_av(i)
            if p3_pipe:
                emit_pt_av(NOWN - 1)
            psA_free_s = evac_done[NOWN - 1]
            k_h_free = P.ctr["t"]
            if hh == 1 and g2 + 1 < p3_groups:
                vprod_done_by_group[g2 + 1] = P.ctr["v"]
        v2_free_par[g2 % 2] = P.ctr["t"]
        kv_buf_free = P.ctr["t"]
    fence()
    attn_v_done = P.ctr["v"]
    attn_t_done = P.ctr["t"]
    attn_s_done = P.ctr["s"]
    if limit == 3:
        return _finish()

    wo_free = {0: attn_t_done, 1: attn_t_done}
    oproj_evac = {}
    for eg in range(4):
        dwo = P.emit("gpsimd", lambda e, eg=eg: e.dma_start(
            out=wo_sb[eg % 2].ap(), in_=wo_t[eg]),
            waits=[("t", wo_free[eg % 2])], inc="dwo")
        for tc in range(NOWN):
            bank = tc % 4
            ps = psA.ap()[:, bank * 512:(bank + 1) * 512]
            for j in range(16):
                w = [("dwo", dwo), ("s", max(oproj_evac.get((eg, tc - 4), 0),
                                             psA_free_s, attn_s_done)),
                     ("v", attn_v_done)]
                P.emit("tensor", lambda e, ps=ps, j=j, tc=tc, eg=eg: e.matmul(
                    ps,
                    AVALL.ap()[:, j * 1024 + tc * 128: j * 1024 + (tc + 1) * 128],
                    wo_sb[eg % 2].ap()[:, j * 512:(j + 1) * 512],
                    start=(j == 0), stop=(j == 15)),
                    waits=w if j == 0 else (), inc="t" if j == 15 else None)
            to = P.ctr["t"]
            oproj_evac[(eg, tc)] = P.emit(
                "scalar", lambda e, ps=ps, tc=tc, eg=eg: e.activation(
                    o_full.ap()[:, tc * DM + eg * 512: tc * DM + (eg + 1) * 512],
                    ps, AF.Copy), waits=[("t", to)], inc="s")
        wo_free[eg % 2] = P.ctr["t"]
    oproj_t_done = P.ctr["t"]
    oproj_s_done = P.ctr["s"]

    ln2_done = {}
    for tc in range(NOWN):
        par = tc % 2
        osrc = o_full.ap()[:, tc * DM:(tc + 1) * DM]
        dxc = P.emit("sync", lambda e, tc=tc: e.dma_start(
            out=xstage2.ap(), in_=x_own[tc * 128:(tc + 1) * 128, :]),
            waits=[("t", attn_t_done), ("v", ln2_done.get(tc - 1, 0))],
            inc="dx2")
        fw = [("s", oproj_evac[(3, tc)]), ("t", oproj_t_done)]
        o, ss, vd = ln_stats(osrc, psA.ap()[:, 0:DM], par, fw)
        xr = x_res.ap()[:, tc * DM:(tc + 1) * DM]
        vstt = P.emit("vector", lambda e, xr=xr, osrc=osrc, o=o:
            e.scalar_tensor_tensor(
                xr, osrc, stats.ap()[:, C_RSTD + o:C_RSTD + o + 1], xstage2.ap(),
                op0=ALU.mult, op1=ALU.add),
            waits=[("v", vd), ("s", ss), ("dx2", dxc), ("t", oproj_t_done)],
            inc="v")
        ln2_done[tc] = P.emit("vector", lambda e, xr=xr, o=o: e.tensor_scalar(
            xr, xr, stats.ap()[:, C_NMR + o:C_NMR + o + 1], None, op0=ALU.add),
            waits=[("v", vstt)], inc="v")
    fence()
    ln2_v_done = P.ctr["v"]
    if limit == 4:
        return _finish()

    ln3_tr_t = 0
    for tc in range(NOWN):
        par = tc % 2
        xr = x_res.ap()[:, tc * DM:(tc + 1) * DM]
        o, ss, vd = ln_stats(xr, psA.ap()[:, 0:DM], par,
                             [("v", ln2_done[tc]), ("t", oproj_t_done)])
        xnb = xstage2.bitcast(BF16).ap()[:, 0:DM]
        nd = P.emit("scalar", lambda e, xr=xr, o=o, xnb=xnb: e.activation(
            xnb, xr, AF.Identity,
            bias=stats.ap()[:, C_NMR + o:C_NMR + o + 1],
            scale=stats.ap()[:, C_RSTD + o:C_RSTD + o + 1]),
            waits=[("v", vd), ("t", ln3_tr_t)], inc="s")
        for bt in range(4):
            tiles = [xnb[:, (bt * 4 + k) * 128:(bt * 4 + k + 1) * 128]
                     for k in range(4)]
            dst = XT3r[:, bt * 4:bt * 4 + 4, tc * 128:(tc + 1) * 128]
            transpose_batch(tiles, dst, half=bt % 2, t_waits=[("s", nd)],
                            evac_waits=[("t", attn_t_done)])
        ln3_tr_t = P.ctr["t"]
    fence()
    ln3_t_done = P.ctr["t"]
    ln3_v_done = P.ctr["v"]
    ln3_s_done = P.ctr["s"]
    if limit == 5:
        return _finish()

    sdb_dw = P.emit("gpsimd", lambda e: e.dma_start(out=sdb.ap(), in_=sdb_d),
                    waits=[("v", attn_v_done)], inc="dwo")
    gu_free = {0: oproj_t_done, 1: oproj_t_done}
    fuse_read = 0
    silu_free = ln3_s_done
    wd_free = {0: oproj_t_done, 1: oproj_t_done}
    fuse_written = ln3_v_done
    fuse_written_init = ln3_v_done
    silu_par = {0: ln3_s_done, 1: ln3_s_done}
    fuse_par = {0: ln3_v_done, 1: ln3_v_done}
    y_mult_done = 0
    ln4_v_done = 0
    ln4_sq_s = ln3_s_done
    out_dx = []

    for hf in range(2):
        for c in range(64):
            buf = c % 2
            dgu = P.emit("sync", lambda e, c=c, buf=buf: e.dma_start(
                out=wgu_sb[buf].ap(), in_=wgu_bf[c]),
                waits=[("t", gu_free[buf]), ("t", fuse_read),
                       ("v", y_mult_done), ("s", ln4_sq_s),
                       ("dpc", dpc_done)], inc="dgu")
            for j in range(16):
                w = [("dgu", dgu), ("s", max(silu_free, ln3_s_done, ln4_sq_s)),
                     ("v", fuse_written)]
                P.emit("tensor", lambda e, j=j, buf=buf, hf=hf: e.matmul(
                    psB.ap()[:, 0:512], wgu_sb[buf].ap()[:, j * 128:(j + 1) * 128],
                    XT3r[:, j, hf * 512:(hf + 1) * 512],
                    start=(j == 0), stop=(j == 15)), waits=w if j == 0 else ())
                P.emit("tensor", lambda e, j=j, buf=buf, hf=hf: e.matmul(
                    psD.ap()[:, 0:512], wgu_sb[buf].ap()[:, 2048 + j * 128:2048 + (j + 1) * 128],
                    XT3r[:, j, hf * 512:(hf + 1) * 512],
                    start=(j == 0), stop=(j == 15)),
                    inc="t" if j == 15 else None)
            tgu = P.ctr["t"]
            fslice = fuse.ap()[:, c * 512:(c + 1) * 512]
            silu_free = P.emit("scalar", lambda e, c=c, fs=fslice: e.activation(
                fs, psB.ap()[:, 0:512], AF.Silu,
                bias=0.0, scale=sg_sb.ap()[:, c:c + 1]),
                waits=[("t", tgu), ("v", ln2_v_done), ("t", fuse_read)], inc="s")
            fuse_written = P.emit("vector", lambda e, c=c, fs=fslice:
                                  e.scalar_tensor_tensor(
                                      fs, psD.ap()[:, 0:512],
                                      su_sb.ap()[:, c:c + 1], fs,
                                      op0=ALU.mult, op1=ALU.mult),
                                  waits=[("s", silu_free)], inc="v")
            gu_free[buf] = P.ctr["t"]
        fuse_v_done = P.ctr["v"]

        y_bufs = (y_buf, y_buf2)
        for pr2 in range(2):
            tcs = (2 * pr2, 2 * pr2 + 1)
            ln4_of_pair = []
            for hg in range(4):
                banks = [t2i * 2 + hg % 2 for t2i in range(2)]
                for og in range(8):
                    buf = og % 2
                    dwd = P.emit("sync", lambda e, hg=hg, og=og, buf=buf:
                                 e.dma_start(out=wd_sb[buf].ap(),
                                             in_=wd_bf[hg, og]),
                                 waits=[("t", wd_free[buf]),
                                        ("dpc", dpc_done)], inc="dwd")
                    for o8 in range(8):
                        c = og * 8 + o8
                        w = [("dwd", dwd), ("v", fuse_v_done),
                             ("v", y_mult_done),
                             ("s", max(ln4_sq_s, silu_free))]
                        for t2i, tc2 in enumerate(tcs):
                            psd = psA.ap()[:, banks[t2i] * 512:
                                           (banks[t2i] + 1) * 512]
                            P.emit("tensor",
                                   lambda e, c=c, o8=o8, tc2=tc2, buf=buf,
                                   psd=psd: e.matmul(
                                       psd,
                                       fuse.ap()[:, c * 512 + tc2 * 128:
                                                 c * 512 + (tc2 + 1) * 128],
                                       wd_sb[buf].ap()[:, o8 * 512:(o8 + 1) * 512],
                                       start=(c == 0), stop=(c == 63),
                                       skip_group_check=True),
                                   waits=w if (o8 == 0 and t2i == 0) else (),
                                   inc="t" if (o8 == 7 and t2i == 1) else None)
                    wd_free[buf] = P.ctr["t"]
                td = P.ctr["t"]
                for t2i in range(2):
                    psd = psA.ap()[:, banks[t2i] * 512:(banks[t2i] + 1) * 512]
                    P.emit("vector", lambda e, psd=psd, hg=hg, t2i=t2i:
                           e.tensor_mul(
                               y_bufs[t2i].ap()[:, hg * 512:(hg + 1) * 512], psd,
                               sdb.ap()[:, hg * 512:(hg + 1) * 512]),
                           waits=[("t", td), ("dwo", sdb_dw), ("s", ln4_sq_s)],
                           inc="v" if (hg == 3 and t2i == 1) else None)
            y_mult_done = P.ctr["v"]
            for t2i, tc2 in enumerate(tcs):
                slot = hf * 4 + tc2
                par = tc2 % 2
                o, ss, vd = ln_stats(y_bufs[t2i].ap(), psA.ap()[:, 0:DM], par,
                                     [("v", y_mult_done)])
                ln4_sq_s = P.ctr["s"]
                xr = x_res.ap()[:, slot * DM:(slot + 1) * DM]
                vstt4 = P.emit("vector", lambda e, xr=xr, o=o, t2i=t2i:
                    e.scalar_tensor_tensor(
                        xr, y_bufs[t2i].ap(),
                        stats.ap()[:, C_RSTD + o:C_RSTD + o + 1], xr,
                        op0=ALU.mult, op1=ALU.add),
                    waits=[("v", vd), ("s", ss), ("v", ln3_v_done)], inc="v")
                vfin = P.emit("vector", lambda e, xr=xr, o=o: e.tensor_scalar(
                    xr, xr, stats.ap()[:, C_NMR + o:C_NMR + o + 1], None,
                    op0=ALU.add), waits=[("v", vstt4)], inc="v")
                ln4_v_done = vfin
                out_dx.append(P.emit("sync", lambda e, slot=slot: e.dma_start(
                    out=y_d[slot * 128:(slot + 1) * 128, :],
                    in_=x_res.ap()[:, slot * DM:(slot + 1) * DM]),
                    waits=[("v", vfin)], inc="dout"))
            y_mult_done = P.ctr["v"]
        fuse_read = P.ctr["t"]

    dx_final = P.ctr["dout"]
    P.emit("sync", lambda e, v=dx_final: e.wait_ge(sems["dout"], v))

    return _finish()


def _own_rows(r):
    return np.concatenate([np.arange((2 * i + r) * 128, (2 * i + r + 1) * 128)
                           for i in range(NOWN)])


def _rope_tables():
    inv = 1.0 / (10000.0 ** (np.arange(0, HD, 2, dtype=np.float64) / HD))
    ang = np.concatenate([inv, inv])[:, None] * np.arange(S, dtype=np.float64)[None, :]
    cosf = np.cos(ang)
    sinf = np.sin(ang)
    sin2 = sinf.copy()
    sin2[0:64] *= -1.0
    sin2 = np.roll(sin2, 64, axis=0)
    return cosf, sin2


def _masks(r):
    m = np.zeros((NOWN, 128, 256), np.float32)
    tk = np.arange(256)
    p = np.arange(128)
    for i in range(NOWN):
        q = 2 * i + r
        gtq = q * 128 + p
        gtk = 2 * i * 128 + tk
        m[i] = np.where(gtk[None, :] <= gtq[:, None], 0.0, -1e30)
    return np.ascontiguousarray(m.transpose(1, 0, 2).reshape(128, NOWN * 256))


def kernel(L0_attn_x, L0_attn_Wq, L0_attn_Wk, L0_attn_Wv, L0_attn_Wo,
           L0_ffn_w_gate, L0_ffn_w_up, L0_ffn_w_down,
           L0_ffn_s_gate, L0_ffn_s_up, L0_ffn_s_down):
    x = np.ascontiguousarray(np.asarray(L0_attn_x, dtype=np.float32))
    Wq = np.asarray(L0_attn_Wq, dtype=np.float32)
    Wk = np.asarray(L0_attn_Wk, dtype=np.float32)
    Wv = np.asarray(L0_attn_Wv, dtype=np.float32)
    Wo = np.asarray(L0_attn_Wo, dtype=np.float32)
    wg = np.asarray(L0_ffn_w_gate).astype(np.int8)
    wu = np.asarray(L0_ffn_w_up).astype(np.int8)
    wd = np.asarray(L0_ffn_w_down).astype(np.int8)
    sg = np.asarray(L0_ffn_s_gate, dtype=np.float32)
    su = np.asarray(L0_ffn_s_up, dtype=np.float32)
    sd = np.asarray(L0_ffn_s_down, dtype=np.float32)

    def qkv_tile(W):
        return np.ascontiguousarray(
            W.reshape(H, HD, 16, 128).transpose(0, 3, 2, 1))

    wq_t = qkv_tile(Wq)
    wk_t = np.ascontiguousarray(
        qkv_tile(Wk).reshape(8, 2, 128, 16, 128).transpose(0, 2, 1, 3, 4)
        .reshape(8, 128, 4096))
    wv_t = np.ascontiguousarray(
        qkv_tile(Wv).reshape(8, 2, 128, 16, 128).transpose(0, 2, 3, 1, 4)
        .reshape(8, 128, 4096))
    wo_t = np.ascontiguousarray(Wo.reshape(4, 512, 16, 128).transpose(0, 3, 2, 1))
    wg_t = wg.reshape(16, 128, 64, 128).transpose(2, 1, 0, 3).reshape(64, 128, 2048)
    wu_t = wu.reshape(16, 128, 64, 128).transpose(2, 1, 0, 3).reshape(64, 128, 2048)
    wgu_t = np.ascontiguousarray(np.concatenate([wg_t, wu_t], axis=2))
    wd_t = np.ascontiguousarray(
        wd.reshape(8, 8, 128, 4, 512).transpose(3, 0, 2, 1, 4)
        .reshape(4, 8, 128, 4096))
    sg_t = np.ascontiguousarray(sg.reshape(64, 128).T)
    su_t = np.ascontiguousarray(su.reshape(64, 128).T)
    sdb = np.ascontiguousarray(np.broadcast_to(sd[None, :], (128, DM)))

    cosf, sin2 = _rope_tables()
    cosk = np.ascontiguousarray(cosf).astype(BF)
    sink2 = np.ascontiguousarray(sin2).astype(BF)
    identb = np.eye(128, dtype=BF)
    identf = np.eye(128, dtype=np.float32)
    ones = np.ones((1, 128), np.float32)

    nc = build_program()

    rows_by_role = [_own_rows(0), _own_rows(1)]
    role_data = []
    for r in (0, 1):
        rows = rows_by_role[r]
        role_data.append({
            "cosq": np.ascontiguousarray(cosf[:, rows] * ATT_SCALE).astype(BF),
            "sinq2": np.ascontiguousarray(sin2[:, rows] * ATT_SCALE).astype(BF),
            "maskq": _masks(r),
        })

    in_maps = []
    for c in range(8):
        b, r = c // 2, c % 2
        rows = rows_by_role[r]
        in_maps.append({
            "x_full": x[b],
            "x_own": np.ascontiguousarray(x[b][rows]),
            "wq_t": wq_t, "wk_t": wk_t, "wv_t": wv_t, "wo_t": wo_t,
            "wgu_t": wgu_t, "wd_t": wd_t,
            "sg": sg_t, "su": su_t, "sdb": sdb,
            "cosk": cosk, "sink2": sink2,
            "cosq": role_data[r]["cosq"], "sinq2": role_data[r]["sinq2"],
            "maskq": role_data[r]["maskq"],
            "identb": identb, "identf": identf, "ones": ones,
        })

    res = run_bass_kernel_spmd(nc, in_maps, core_ids=list(range(8)))

    out = np.empty((NB, S, DM), np.float32)
    for c in range(8):
        b, r = c // 2, c % 2
        out[b][rows_by_role[r]] = np.asarray(res.results[c]["y"])
    return out


# version 17
# speedup vs baseline: 1.1471x; 1.1471x over previous
import numpy as np
import ml_dtypes

import concourse.bass as bass
import concourse.mybir as mybir
from concourse.bass_utils import run_bass_kernel_spmd

F32 = mybir.dt.float32
BF16 = mybir.dt.bfloat16
I8 = mybir.dt.int8
AF = mybir.ActivationFunctionType
ALU = mybir.AluOpType
AX = mybir.AxisListType
BF = ml_dtypes.bfloat16

S = 2048
DM = 2048
H = 16
HD = 128
INTER = 8192
NB = 4
NCH = S // 128
NOWN = 8
EPS = 1e-5
ATT_SCALE = HD ** -0.5

_SKIP_EVAC = [False]

C_SUM, C_NM, C_M2E, C_VPE, C_RC, C_RSTD, C_NMR, C_SUM2 = range(8)


class Prog:

    ENGINES = ("sync", "gpsimd", "tensor", "vector", "scalar")
    OWN = {"tensor": "t", "vector": "v", "scalar": "s"}
    DMA = ("dc", "dxa", "dxb", "dx2", "dout", "dwq", "dwk", "dwo", "dgu", "dwd", "dpc")

    def __init__(self, sems):
        self.sems = sems
        self.ctr = {k: 0 for k in sems}
        self.streams = {e: [] for e in self.ENGINES}

    def emit(self, engine, fn, waits=(), inc=None, amt=None):
        w = []
        for sname, v in waits:
            if v is None or v <= 0:
                continue
            assert v <= self.ctr[sname], (engine, sname, v, self.ctr[sname])
            if sname in self.DMA:
                assert v == self.ctr[sname], ("partial DMA wait", engine, sname,
                                              v, self.ctr[sname])
            w.append((sname, v))
        post = None
        incs = ()
        if inc is not None:
            if amt is None:
                amt = 16 if inc in self.DMA else 1
            self.ctr[inc] += amt
            post = self.ctr[inc]
            incs = ((inc, amt),)
        self.streams[engine].append((tuple(w), fn, incs))
        return post

    def replay(self, engine, eng):
        last = {}
        for waits, fn, incs in self.streams[engine]:
            for sname, v in waits:
                if last.get(sname, 0) >= v:
                    continue
                eng.wait_ge(self.sems[sname], v)
                last[sname] = v
            inst = fn(eng)
            for sname, amt in incs:
                inst.then_inc(self.sems[sname], amt)


def build_program(limit=None, taps=(), p3_groups=8, p3_pipe=True):
    nc = bass.Bass("TRN2", target_bir_lowering=False,
                   detect_race_conditions=False)

    x_full = nc.dram_tensor("x_full", [S, DM], F32, kind="ExternalInput").ap()
    x_own = nc.dram_tensor("x_own", [NOWN * 128, DM], F32, kind="ExternalInput").ap()
    wq_t = nc.dram_tensor("wq_t", [H, 128, 16, 128], F32, kind="ExternalInput").ap()
    wk_t = nc.dram_tensor("wk_t", [8, 128, 4096], F32, kind="ExternalInput").ap()
    wv_t = nc.dram_tensor("wv_t", [8, 128, 4096], F32, kind="ExternalInput").ap()
    wo_t = nc.dram_tensor("wo_t", [4, 128, 16, 512], F32, kind="ExternalInput").ap()
    wgu_t = nc.dram_tensor("wgu_t", [64, 128, 4096], I8, kind="ExternalInput").ap()
    wd_t = nc.dram_tensor("wd_t", [4, 8, 128, 4096], I8, kind="ExternalInput").ap()
    sg_d = nc.dram_tensor("sg", [128, 64], F32, kind="ExternalInput").ap()
    su_d = nc.dram_tensor("su", [128, 64], F32, kind="ExternalInput").ap()
    sdb_d = nc.dram_tensor("sdb", [128, DM], F32, kind="ExternalInput").ap()
    cosk_d = nc.dram_tensor("cosk", [128, S], BF16, kind="ExternalInput").ap()
    sink_d = nc.dram_tensor("sink2", [128, S], BF16, kind="ExternalInput").ap()
    cosq_d = nc.dram_tensor("cosq", [128, 1024], BF16, kind="ExternalInput").ap()
    sinq_d = nc.dram_tensor("sinq2", [128, 1024], BF16, kind="ExternalInput").ap()
    mask_d = nc.dram_tensor("maskq", [128, NOWN * 256], F32, kind="ExternalInput").ap()
    idb_d = nc.dram_tensor("identb", [128, 128], BF16, kind="ExternalInput").ap()
    idf_d = nc.dram_tensor("identf", [128, 128], F32, kind="ExternalInput").ap()
    ones_d = nc.dram_tensor("ones", [1, 128], F32, kind="ExternalInput").ap()
    y_d = nc.dram_tensor("y", [NOWN * 128, DM], F32, kind="ExternalOutput").ap()

    SBASE = 16512
    A0, B0, C0, D0, E0 = (SBASE, SBASE + 65536, SBASE + 98304,
                          SBASE + 131072, SBASE + 196608)

    def at(name, shape, dtype, off):
        return nc.alloc_sbuf_tensor_at(name, shape, dtype, offset=off)

    XT1 = at("XT1", [128, 16 * S], BF16, A0)
    o_full = at("o_full", [128, NOWN * DM], F32, A0)
    fuse = at("fuse", [128, 64 * 512], BF16, A0)

    XTQ = at("XTQ", [128, 16 * 1024], BF16, B0)
    mask_sb = at("mask_sb", [128, NOWN * 256], F32, B0)
    cosk = at("cosk_sb", [128, S], BF16, B0 + 8192)
    sink2 = at("sink_sb", [128, S], BF16, B0 + 12288)
    wk2 = at("wk2", [128, 2 * 16 * 128], BF16, B0 + 16384)
    wv2 = at("wv2", [128, 16 * 256], BF16, B0 + 24576)
    XT3 = at("XT3", [128, 16 * 1024], BF16, B0)

    qall = at("qall", [128, H * 1024], BF16, C0)
    wo_sb = [at(f"wo{i}", [128, 16 * 512], BF16, C0 + i * 16384) for i in (0, 1)]
    wgu_sb = [at(f"wgu{i}", [128, 2 * 16 * 128], BF16, C0 + i * 8192) for i in (0, 1)]
    wd_sb = [at(f"wd{i}", [128, 8 * 512], BF16, C0 + 16384 + i * 8192) for i in (0, 1)]

    xstage = at("xstage", [128, 2 * DM], F32, D0)
    xn = at("xn", [128, 2 * DM], BF16, D0 + 16384)
    wq_sb = [at(f"wq{i}", [128, 16 * 128], BF16, D0 + 24576 + i * 4096) for i in (0, 1)]
    v2_sb = at("v2", [128, 16 * 256], BF16, D0)
    v2_sb2 = at("v2b", [128, 16 * 256], BF16, D0 + 57344)
    k_h = at("k_h", [128, S], BF16, D0 + 8192)
    PT_sb = at("PT", [128, 2 * 1024], BF16, D0 + 12288)
    qsb = at("qsb", [128, 512], BF16, D0 + 32768)
    ksb = at("ksb", [128, 512], BF16, D0 + 56320)
    rope_tmp = at("rope_tmp", [128, 512], BF16, D0 + 53248)
    AVALL = at("AVALL", [128, H * 1024], BF16, D0 + 20480)
    x_res = at("x_res", [128, NOWN * DM], F32, D0)

    p_row = at("p_row", [128, S], BF16, E0)
    p_row2 = at("p_row2", [128, S], BF16, E0 + 4096)
    cosq = at("cosq_sb", [128, 1024], BF16, E0 + 4096)
    sinq2 = at("sinq_sb", [128, 1024], BF16, E0 + 6144)
    xstage2 = at("xstage2", [128, DM], F32, E0)
    y_buf = at("y_buf", [128, DM], F32, E0)
    bc_sb = at("bc_sb", [128, 512], F32, E0 + 8192)
    sdb = at("sdb_sb", [128, DM], BF16, E0 + 8192)
    identb = at("identb_sb", [128, 128], BF16, E0 + 12288)
    identf = at("identf_sb", [128, 128], F32, E0 + 12544)
    ones_sb = at("ones_sb", [1, 128], F32, E0 + 13056)
    sg_sb = at("sg_sb", [128, 64], F32, E0 + 13568)
    su_sb = at("su_sb", [128, 64], F32, E0 + 13824)
    stats = at("stats", [128, 16], F32, E0 + 14080)
    sums8 = at("sums8", [128, 8], F32, E0 + 14144)
    rcp8 = at("rcp8", [128, 8], F32, E0 + 14176)
    nmcol = at("nmcol", [128, 1], F32, E0 + 14208)
    negshift = at("negshift", [128, 1], F32, E0 + 14240)
    rrow = at("rrow", [1, 512], F32, D0 + 54272)
    y_buf2 = at("y_buf2", [128, DM], F32, C0)

    psA = nc.alloc_psum_tensor("psA", [128, 2048], F32)
    psB = nc.alloc_psum_tensor("psB", [128, 512], F32)
    psC = nc.alloc_psum_tensor("psC", [128, 2048], BF16)
    psD = nc.alloc_psum_tensor("psD", [128, 512], F32)

    from contextlib import ExitStack
    ctx = ExitStack()
    sems = {}
    for sname in Prog.DMA + ("t", "v", "s"):
        sems[sname] = ctx.enter_context(nc.semaphore("sem_" + sname))
    P = Prog(sems)

    def _finish():
        for tname in taps:
            th = tapreg[tname]
            shape = list(th.shape)
            td = nc.dram_tensor("tap_" + tname, shape, th.dtype,
                                kind="ExternalOutput").ap()
            P.emit("sync", lambda e, th=th, td=td: e.dma_start(
                out=td, in_=th.ap()),
                waits=[("t", P.ctr["t"]), ("v", P.ctr["v"]), ("s", P.ctr["s"])],
                inc="dout")
        for eng, semlist in (("sync", ("dc", "dxa", "dxb", "dx2", "dout")),
                             ("gpsimd", ("dwq", "dwk", "dwo", "dgu", "dwd"))):
            for sname in semlist:
                v = P.ctr[sname]
                if v:
                    P.emit(eng, lambda e, s=sname, v=v: e.wait_ge(sems[s], v))
        with nc.Block() as block:
            block.sync(lambda e: P.replay("sync", e))
            block.gpsimd(lambda e: P.replay("gpsimd", e))
            block.tensor(lambda e: P.replay("tensor", e))
            block.vector(lambda e: P.replay("vector", e))
            block.scalar(lambda e: P.replay("scalar", e))
        ctx.close()
        return nc


    XT1r = XT1.ap().rearrange("p (j t) -> p j t", j=16)
    XTQr = XTQ.ap().rearrange("p (j t) -> p j t", j=16)
    XT3r = XT3.ap().rearrange("p (j t) -> p j t", j=16)
    psCr = psC.ap().rearrange("p (n t) -> p n t", n=16)

    def ln_stats(src_ap, sq_out_ap, par, first_waits):
        o = 8 * par
        st = stats.ap()
        v0 = P.emit("vector", lambda e, a=src_ap, o=o: e.reduce_sum(
            st[:, o + C_SUM:o + C_SUM + 1], a, axis=AX.X), waits=first_waits,
            inc="v")
        sq = P.emit("scalar", lambda e, a=src_ap, b=sq_out_ap, o=o: e.activation(
            b, a, AF.Square, bias=0.0, scale=1.0,
            accum_out=st[:, o + C_SUM2:o + C_SUM2 + 1]),
            waits=first_waits, inc="s")
        v1 = P.emit("vector", lambda e, o=o: e.tensor_scalar(
            st[:, o + C_NM:o + C_NM + 1], st[:, o + C_SUM:o + C_SUM + 1],
            -1.0 / DM, None, op0=ALU.mult), waits=[("v", v0)], inc="v")
        v2 = P.emit("vector", lambda e, o=o: e.tensor_scalar(
            st[:, o + C_M2E:o + C_M2E + 1], st[:, o + C_NM:o + C_NM + 1],
            st[:, o + C_NM:o + C_NM + 1], EPS, op0=ALU.mult, op1=ALU.subtract),
            waits=[("v", v1)], inc="v")
        v3 = P.emit("vector", lambda e, o=o: e.tensor_scalar(
            st[:, o + C_VPE:o + C_VPE + 1], st[:, o + C_SUM2:o + C_SUM2 + 1],
            1.0 / DM, st[:, o + C_M2E:o + C_M2E + 1],
            op0=ALU.mult, op1=ALU.subtract),
            waits=[("s", sq), ("v", v2)], inc="v")
        vv = P.emit("vector", lambda e, o=o: e.reciprocal(
            st[:, o + C_RC:o + C_RC + 1], st[:, o + C_VPE:o + C_VPE + 1]),
            waits=[("v", v3)], inc="v")
        ss = P.emit("scalar", lambda e, o=o: e.activation(
            st[:, o + C_RSTD:o + C_RSTD + 1], st[:, o + C_RC:o + C_RC + 1],
            AF.Sqrt), waits=[("v", vv)], inc="s")
        vd = P.emit("vector", lambda e, o=o: e.tensor_mul(
            st[:, o + C_NMR:o + C_NMR + 1], st[:, o + C_NM:o + C_NM + 1],
            st[:, o + C_RSTD:o + C_RSTD + 1]), waits=[("s", ss)], inc="v")
        return o, ss, vd

    psC_evac = {0: 0, 1: 0}

    def transpose_batch(src_tiles, dst_ap, half, t_waits=(), evac_waits=(),
                        identity=None):
        idt = identity if identity is not None else identb
        n = len(src_tiles)
        tw = list(t_waits)
        if psC_evac[half]:
            tw.append(("v", psC_evac[half]))
        tt = None
        for k, sap in enumerate(src_tiles):
            tt = P.emit("tensor", lambda e, a=sap, k=k, half=half: e.transpose(
                psC.ap()[:, half * 1024 + k * 128: half * 1024 + (k + 1) * 128],
                a, idt.ap()[:, :]), waits=tw if k == 0 else (),
                inc="t" if k == n - 1 else None)
        if _SKIP_EVAC[0]:
            ve = P.emit("vector", lambda e: e.nop(),
                        waits=[("t", tt)] + list(evac_waits), inc="v")
            psC_evac[half] = ve
            return ve
        src = psCr[:, half * 8: half * 8 + n, :]
        ve = P.emit("vector", lambda e, s=src, d=dst_ap: e.tensor_copy(d, s),
                    waits=[("t", tt)] + list(evac_waits), inc="v")
        psC_evac[half] = ve
        return ve

    tapreg = {"XT1": XT1, "XTQ": XTQ, "stats": stats, "qall": qall,
              "k_h": k_h, "v2": v2_sb, "AVALL": AVALL, "o_full": o_full,
              "x_res": x_res, "fuse": fuse, "xn": xn, "p_row": p_row,
              "sums8": sums8, "rcp8": rcp8, "mask_sb": mask_sb, "cosk": cosk,
              "xstage": xstage, "y_buf": y_buf, "XT3": XT3}

    def fence():
        pass

    P.emit("vector", lambda e: e.memset(negshift.ap(), -8.0), inc="v")
    P.emit("sync", lambda e: e.dma_start(out=identb.ap(), in_=idb_d), inc="dc")
    P.emit("sync", lambda e: e.dma_start(out=identf.ap(), in_=idf_d), inc="dc")
    P.emit("sync", lambda e: e.dma_start(out=ones_sb.ap(), in_=ones_d), inc="dc")
    P.emit("sync", lambda e: e.dma_start(out=cosq.ap(), in_=cosq_d), inc="dc")
    P.emit("sync", lambda e: e.dma_start(out=sinq2.ap(), in_=sinq_d), inc="dc")
    P.emit("sync", lambda e: e.dma_start(out=sg_sb.ap(), in_=sg_d), inc="dc")
    dx_const = P.emit("sync", lambda e: e.dma_start(out=su_sb.ap(), in_=su_d),
                      inc="dc")

    norm_done = {}
    trans_t = {}

    def ln_chunk(ci, kind, c):
        par = ci % 2
        src_dram = (x_full if kind == "full" else x_own)[c * 128:(c + 1) * 128, :]
        xs = xstage.ap()[:, par * DM:(par + 1) * DM]
        xnp = xn.ap()[:, par * DM:(par + 1) * DM]
        dsem = "dxa" if par == 0 else "dxb"
        dxc = P.emit("sync", lambda e, a=xs, b=src_dram: e.dma_start(out=a, in_=b),
                     waits=[("s", norm_done.get(ci - 2, 0))], inc=dsem)
        fw = [(dsem, dxc), ("t", trans_t.get(ci - 2, 0))]
        o, ss, vd = ln_stats(xs, xnp, par, fw)
        nd = P.emit("scalar", lambda e, a=xnp, b=xs, o=o: e.activation(
            a, b, AF.Identity,
            bias=stats.ap()[:, o + C_NMR:o + C_NMR + 1],
            scale=stats.ap()[:, o + C_RSTD:o + C_RSTD + 1]),
            waits=[("v", vd)], inc="s")
        norm_done[ci] = nd
        for bt in range(4):
            tiles = [xnp[:, (bt * 4 + k) * 128:(bt * 4 + k + 1) * 128]
                     for k in range(4)]
            r = XT1r if kind == "full" else XTQr
            dst = r[:, bt * 4:bt * 4 + 4, c * 128:(c + 1) * 128]
            transpose_batch(tiles, dst, half=bt % 2, t_waits=[("s", nd)])
        trans_t[ci] = P.ctr["t"]

    for c in range(NOWN):
        ln_chunk(c, "own", c)
    xtq_v_done = P.ctr["v"]

    q_mm_done = {}
    qsb_free = 0
    psQ_free = {0: 0, 1: 0}

    def q_head(h):
        global_unused = None
        wbuf = wq_sb[h % 2]
        w = [("t", q_mm_done.get(h - 2, 0)), ("s", norm_done[NOWN - 1])]
        dwc = P.emit("gpsimd", lambda e, a=wbuf, b=wq_t[h]: e.dma_start(
            out=a.ap(), in_=b), waits=w, inc="dwq")
        nonlocal_state = None
        for G in range(2):
            ps = (psB if G == 0 else psD).ap()[:, 0:512]
            for j in range(16):
                w = [("dwq", dwc), ("s", psQ_free[G]), ("v", xtq_v_done)]
                P.emit("tensor", lambda e, ps=ps, a=wbuf, j=j, G=G: e.matmul(
                    ps, a.ap()[:, j * 128:(j + 1) * 128],
                    XTQr[:, j, G * 512:(G + 1) * 512],
                    start=(j == 0), stop=(j == 15)),
                    waits=w if j == 0 else (), inc="t" if j == 15 else None)
            tq = P.ctr["t"]
            ev = P.emit("scalar", lambda e, ps=ps: e.activation(
                qsb.ap(), ps, AF.Copy), waits=[("t", tq), ("v", q_head.qsb_free)],
                inc="s")
            psQ_free[G] = ev
            qa = qall.ap()[:, h * 1024 + G * 512: h * 1024 + (G + 1) * 512]
            cq = cosq.ap()[:, G * 512:(G + 1) * 512]
            sq2 = sinq2.ap()[:, G * 512:(G + 1) * 512]
            P.emit("vector", lambda e, sq2=sq2: e.tensor_mul(
                rope_tmp.ap()[0:64, :], qsb.ap()[64:128, :], sq2[64:128, :]),
                waits=[("s", ev), ("dc", dx_const)])
            P.emit("vector", lambda e, sq2=sq2: e.tensor_mul(
                rope_tmp.ap()[64:128, :], qsb.ap()[0:64, :], sq2[0:64, :]))
            q_head.qsb_free = P.emit("vector", lambda e, qa=qa, cq=cq: e.tensor_mul(
                qa, qsb.ap(), cq), inc="v")
            P.emit("vector", lambda e, qa=qa: e.tensor_add(
                qa, qa, rope_tmp.ap()), waits=[("v", q_head.qsb_free)], inc="v")
        q_mm_done[h] = P.ctr["t"]
    q_head.qsb_free = 0

    for h in range(H):
        ln_chunk(NOWN + h, "full", h)
        q_head(h)
    ln1_t_done = P.ctr["t"]
    ln1_v_done = P.ctr["v"]
    ln1_s_done = P.ctr["s"]
    if limit == 1:
        return _finish()
    fence()
    p2_t_done = P.ctr["t"]
    p2_s_done = P.ctr["s"]
    p2_v_done = P.ctr["v"]
    if limit == 2:
        return _finish()
    w = [("t", p2_t_done)]
    P.emit("sync", lambda e: e.dma_start(out=mask_sb.ap(), in_=mask_d), waits=w,
           inc="dc")
    P.emit("sync", lambda e: e.dma_start(out=cosk.ap(), in_=cosk_d), inc="dc")
    dx_k = P.emit("sync", lambda e: e.dma_start(out=sink2.ap(), in_=sink_d),
                  inc="dc")
    mask_r = mask_sb.ap().rearrange("p (i c) -> p i c", i=NOWN)

    kv_buf_free = p2_t_done
    v2_free = 0
    k_h_free = 0
    psA_free_s = p2_s_done
    ksb_free = 0
    exp_prev = 0

    class att:
        av_read = 0
        pt_free = {0: 0, 1: 0}
        pt_slot_done = {0: 0, 1: 0}
        psD_free_v = 0
        sums_free = {k: 0 for k in range(8)}

    def vbuf(g2):
        return v2_sb if g2 % 2 == 0 else v2_sb2

    def emit_kv_dma(g2, wk_free_t, v2_free_t):
        P.emit("gpsimd", lambda e, g2=g2: e.dma_start(
            out=wk2.ap(), in_=wk_t[g2]),
            waits=[("t", wk_free_t)], inc="dwk")
        return P.emit("gpsimd", lambda e, g2=g2: e.dma_start(
            out=wv2.ap(), in_=wv_t[g2]),
            waits=[("t", v2_free_t)], inc="dwk")

    def emit_vprod_tc(g2, tc, dwk_val):
        ps = psD.ap()[:, 0:256]
        for j in range(16):
            w = [("dwk", dwk_val), ("v", max(att.vprod_evac, ln1_v_done))]
            P.emit("tensor", lambda e, ps=ps, j=j, tc=tc: e.matmul(
                ps, XT1r[:, j, tc * 128:(tc + 1) * 128],
                wv2.ap()[:, j * 256:(j + 1) * 256],
                start=(j == 0), stop=(j == 15)),
                waits=w if j == 0 else (), inc="t" if j == 15 else None)
        tvp = P.ctr["t"]
        att.vprod_evac = P.emit("vector", lambda e, ps=ps, tc=tc, g2=g2:
            e.tensor_copy(
                vbuf(g2).ap()[:, tc * 256:(tc + 1) * 256], ps),
            waits=[("t", tvp)], inc="v")

    att.vprod_evac = p2_v_done
    dwk_by_group = {0: None}
    vprod_done_by_group = {}
    v2_free_par = {0: 0, 1: 0}

    for g2 in range(p3_groups):
        if g2 == 0:
            dwk_by_group[0] = emit_kv_dma(0, kv_buf_free, 0)
            for tc in range(NCH):
                emit_vprod_tc(0, tc, dwk_by_group[0])
            vprod_done_by_group[0] = P.ctr["v"]
        vprod_done_s = vprod_done_by_group[g2]

        for hh in range(2):
            h = 2 * g2 + hh
            krope_g = {}
            for g in range(4):
                ps = psA.ap()[:, g * 512:(g + 1) * 512]
                for j in range(16):
                    w = [("dwk", dwk_by_group[g2]), ("s", psA_free_s),
                         ("v", ln1_v_done), ("s", ln1_s_done)]
                    P.emit("tensor", lambda e, ps=ps, j=j, hh=hh, g=g: e.matmul(
                        ps,
                        wk2.ap()[:, hh * 2048 + j * 128: hh * 2048 + (j + 1) * 128],
                        XT1r[:, j, g * 512:(g + 1) * 512],
                        start=(j == 0), stop=(j == 15)),
                        waits=w if j == 0 else (), inc="t" if j == 15 else None)
                tk = P.ctr["t"]
                ev = P.emit("scalar", lambda e, ps=ps: e.activation(
                    ksb.ap(), ps, AF.Copy),
                    waits=[("t", tk), ("v", ksb_free)], inc="s")
                kh = k_h.ap()[:, g * 512:(g + 1) * 512]
                ck = cosk.ap()[:, g * 512:(g + 1) * 512]
                sk = sink2.ap()[:, g * 512:(g + 1) * 512]
                P.emit("vector", lambda e, sk=sk: e.tensor_mul(
                    rope_tmp.ap()[0:64, :], ksb.ap()[64:128, :], sk[64:128, :]),
                    waits=[("s", ev), ("dc", dx_k), ("t", k_h_free)])
                P.emit("vector", lambda e, sk=sk: e.tensor_mul(
                    rope_tmp.ap()[64:128, :], ksb.ap()[0:64, :], sk[0:64, :]))
                ksb_free = P.emit("vector", lambda e, kh=kh, ck=ck: e.tensor_mul(
                    kh, ksb.ap(), ck), inc="v")
                krope_g[g] = P.emit("vector", lambda e, kh=kh: e.tensor_add(
                    kh, kh, rope_tmp.ap()), waits=[("v", ksb_free)], inc="v")
            krope_v = P.ctr["v"]
            psA_scores_wait = P.ctr["s"]
            if hh == 1 and g2 + 1 < p3_groups:
                dwk_by_group[g2 + 1] = emit_kv_dma(
                    g2 + 1, P.ctr["t"], v2_free_par[(g2 + 1) % 2])

            exp_done = {}
            evac_done = {}
            sc_done = {}

            def emit_pt_av(i):
                nt = 2 * i + 2
                G = i // 4
                ed = exp_done[i]
                pr = (p_row if i % 2 == 0 else p_row2).ap()
                batches = [list(range(cb * 8, min(cb * 8 + 8, nt)))
                           for cb in range((nt + 7) // 8)]
                ves = []
                for cb, cs in enumerate(batches):
                    half = cb % 2
                    tiles = [pr[:, c * 128:(c + 1) * 128] for c in cs]
                    dst = PT_sb.ap()[:, half * 1024: half * 1024 + len(cs) * 128]
                    ves.append(transpose_batch(
                        tiles, dst, half=half, t_waits=[("v", sc_done[i])],
                        evac_waits=[("t", att.pt_free[half])]))
                for cb, cs in enumerate(batches):
                    half = cb % 2
                    for c in cs:
                        w = [("v", max(ves[cb], vprod_done_s)),
                             ("v", att.av_read)]
                        P.emit("tensor",
                               lambda e, c=c, cb=cb, i=i, half=half, nt=nt,
                               hh=hh, g2=g2: e.matmul(
                                   psB.ap()[:, (i % 4) * 128:(i % 4 + 1) * 128],
                                   vbuf(g2).ap()[:, c * 256 + hh * 128:
                                                 c * 256 + (hh + 1) * 128],
                                   PT_sb.ap()[:, half * 1024 + (c - cb * 8) * 128:
                                              half * 1024 + (c - cb * 8 + 1) * 128],
                                   start=(i % 4 == 0 and c == 0),
                                   stop=(i % 4 == 3 and c == nt - 1),
                                   skip_group_check=True),
                               waits=w if c == cs[0] else (),
                               inc="t" if c == cs[-1] else None)
                    att.pt_free[half] = P.ctr["t"]
                att.pt_slot_done[i % 2] = P.ctr["t"]
                if i % 4 == 3:
                    av_t = P.ctr["t"]
                    att.av_read = P.emit("vector", lambda e, G=G, h=h:
                        e.tensor_copy(
                            AVALL.ap()[:, h * 1024 + G * 512:
                                       h * 1024 + (G + 1) * 512],
                            psB.ap()[:, 0:512]),
                        waits=[("t", av_t)], inc="v")

            for i in range(NOWN):
                W = (2 * i + 2) * 128
                ng = (W + 511) // 512
                for g in range(ng):
                    n = min(512, W - g * 512)
                    w = [("v", krope_g[ng - 1]),
                         ("s", evac_done.get(i - 1, psA_scores_wait))]
                    P.emit("tensor", lambda e, g=g, n=n, h=h, i=i: e.matmul(
                        psA.ap()[:, g * 512:g * 512 + n],
                        qall.ap()[:, h * 1024 + i * 128: h * 1024 + (i + 1) * 128],
                        k_h.ap()[:, g * 512:g * 512 + n],
                        start=True, stop=True),
                        waits=w if g == 0 else (), inc="t" if g == ng - 1 else None)
                tsc = P.ctr["t"]
                if p3_pipe and i > 0:
                    emit_pt_av(i - 1)
                vmask = P.emit("vector", lambda e, W=W, i=i: e.tensor_add(
                    psA.ap()[:, W - 256:W], psA.ap()[:, W - 256:W],
                    mask_r[:, i, :]), waits=[("t", tsc)], inc="v")
                pr = (p_row if i % 2 == 0 else p_row2).ap()
                ed = P.emit("scalar", lambda e, W=W, i=i, pr=pr:
                    e.activation(
                        pr[:, 0:W], psA.ap()[:, 0:W], AF.Exp,
                        bias=negshift.ap()[:, 0:1], scale=1.0,
                        accum_out=sums8.ap()[:, i % 8:i % 8 + 1]),
                    waits=[("v", vmask), ("t", att.pt_slot_done[i % 2]),
                           ("v", att.sums_free[i % 8])], inc="s")
                exp_done[i] = ed
                evac_done[i] = ed
                exp_prev = ed
                rc = P.emit("vector", lambda e, i=i: e.reciprocal(
                    rcp8.ap()[:, i % 8:i % 8 + 1],
                    sums8.ap()[:, i % 8:i % 8 + 1]), waits=[("s", ed)], inc="v")
                att.sums_free[i % 8] = rc
                sc_done[i] = P.emit("vector", lambda e, i=i, pr=pr, W=W:
                    e.tensor_scalar(
                        pr[:, 0:W], pr[:, 0:W],
                        rcp8.ap()[:, i % 8:i % 8 + 1], None, op0=ALU.mult),
                    waits=[("v", rc)], inc="v")
                if hh == 1 and g2 + 1 < p3_groups:
                    emit_vprod_tc(g2 + 1, 2 * i, dwk_by_group[g2 + 1])
                    emit_vprod_tc(g2 + 1, 2 * i + 1, dwk_by_group[g2 + 1])
                if not p3_pipe:
                    emit_pt_av(i)
            if p3_pipe:
                emit_pt_av(NOWN - 1)
            psA_free_s = evac_done[NOWN - 1]
            k_h_free = P.ctr["t"]
            if hh == 1 and g2 + 1 < p3_groups:
                vprod_done_by_group[g2 + 1] = P.ctr["v"]
        v2_free_par[g2 % 2] = P.ctr["t"]
        kv_buf_free = P.ctr["t"]
    fence()
    attn_v_done = P.ctr["v"]
    attn_t_done = P.ctr["t"]
    attn_s_done = P.ctr["s"]
    if limit == 3:
        return _finish()

    wo_free = {0: attn_t_done, 1: attn_t_done}
    oproj_evac = {}
    for eg in range(4):
        dwo = P.emit("gpsimd", lambda e, eg=eg: e.dma_start(
            out=wo_sb[eg % 2].ap(), in_=wo_t[eg]),
            waits=[("t", wo_free[eg % 2])], inc="dwo")
        for tc in range(NOWN):
            bank = tc % 4
            ps = psA.ap()[:, bank * 512:(bank + 1) * 512]
            for j in range(16):
                w = [("dwo", dwo), ("s", max(oproj_evac.get((eg, tc - 4), 0),
                                             psA_free_s, attn_s_done)),
                     ("v", attn_v_done)]
                P.emit("tensor", lambda e, ps=ps, j=j, tc=tc, eg=eg: e.matmul(
                    ps,
                    AVALL.ap()[:, j * 1024 + tc * 128: j * 1024 + (tc + 1) * 128],
                    wo_sb[eg % 2].ap()[:, j * 512:(j + 1) * 512],
                    start=(j == 0), stop=(j == 15)),
                    waits=w if j == 0 else (), inc="t" if j == 15 else None)
            to = P.ctr["t"]
            oproj_evac[(eg, tc)] = P.emit(
                "scalar", lambda e, ps=ps, tc=tc, eg=eg: e.activation(
                    o_full.ap()[:, tc * DM + eg * 512: tc * DM + (eg + 1) * 512],
                    ps, AF.Copy), waits=[("t", to)], inc="s")
        wo_free[eg % 2] = P.ctr["t"]
    oproj_t_done = P.ctr["t"]
    oproj_s_done = P.ctr["s"]

    ln2_done = {}
    for tc in range(NOWN):
        par = tc % 2
        osrc = o_full.ap()[:, tc * DM:(tc + 1) * DM]
        dxc = P.emit("sync", lambda e, tc=tc: e.dma_start(
            out=xstage2.ap(), in_=x_own[tc * 128:(tc + 1) * 128, :]),
            waits=[("t", attn_t_done), ("v", ln2_done.get(tc - 1, 0))],
            inc="dx2")
        fw = [("s", oproj_evac[(3, tc)]), ("t", oproj_t_done)]
        o, ss, vd = ln_stats(osrc, psA.ap()[:, 0:DM], par, fw)
        xr = x_res.ap()[:, tc * DM:(tc + 1) * DM]
        vstt = P.emit("vector", lambda e, xr=xr, osrc=osrc, o=o:
            e.scalar_tensor_tensor(
                xr, osrc, stats.ap()[:, C_RSTD + o:C_RSTD + o + 1], xstage2.ap(),
                op0=ALU.mult, op1=ALU.add),
            waits=[("v", vd), ("s", ss), ("dx2", dxc), ("t", oproj_t_done)],
            inc="v")
        ln2_done[tc] = P.emit("vector", lambda e, xr=xr, o=o: e.tensor_scalar(
            xr, xr, stats.ap()[:, C_NMR + o:C_NMR + o + 1], None, op0=ALU.add),
            waits=[("v", vstt)], inc="v")
    fence()
    ln2_v_done = P.ctr["v"]
    if limit == 4:
        return _finish()

    ln3_tr_t = 0
    for tc in range(NOWN):
        par = tc % 2
        xr = x_res.ap()[:, tc * DM:(tc + 1) * DM]
        o, ss, vd = ln_stats(xr, psA.ap()[:, 0:DM], par,
                             [("v", ln2_done[tc]), ("t", oproj_t_done)])
        xnb = xstage2.bitcast(BF16).ap()[:, 0:DM]
        nd = P.emit("scalar", lambda e, xr=xr, o=o, xnb=xnb: e.activation(
            xnb, xr, AF.Identity,
            bias=stats.ap()[:, C_NMR + o:C_NMR + o + 1],
            scale=stats.ap()[:, C_RSTD + o:C_RSTD + o + 1]),
            waits=[("v", vd), ("t", ln3_tr_t)], inc="s")
        for bt in range(4):
            tiles = [xnb[:, (bt * 4 + k) * 128:(bt * 4 + k + 1) * 128]
                     for k in range(4)]
            dst = XT3r[:, bt * 4:bt * 4 + 4, tc * 128:(tc + 1) * 128]
            transpose_batch(tiles, dst, half=bt % 2, t_waits=[("s", nd)],
                            evac_waits=[("t", attn_t_done)])
        ln3_tr_t = P.ctr["t"]
    fence()
    ln3_t_done = P.ctr["t"]
    ln3_v_done = P.ctr["v"]
    ln3_s_done = P.ctr["s"]
    if limit == 5:
        return _finish()

    sdb_dw = P.emit("gpsimd", lambda e: e.dma_start(out=sdb.ap(), in_=sdb_d),
                    waits=[("v", attn_v_done)], inc="dwo")
    gu_free = {0: oproj_t_done, 1: oproj_t_done}
    fuse_read = 0
    silu_free = ln3_s_done
    wd_free = {0: oproj_t_done, 1: oproj_t_done}
    fuse_written = ln3_v_done
    fuse_written_init = ln3_v_done
    silu_par = {0: ln3_s_done, 1: ln3_s_done}
    fuse_par = {0: ln3_v_done, 1: ln3_v_done}
    y_mult_done = 0
    ln4_v_done = 0
    ln4_sq_s = ln3_s_done
    out_dx = []

    for hf in range(2):
        for c in range(64):
            buf = c % 2
            dgu = P.emit("gpsimd", lambda e, c=c, buf=buf: e.dma_start(
                out=wgu_sb[buf].ap(), in_=wgu_t[c]),
                waits=[("t", gu_free[buf]), ("t", fuse_read),
                       ("v", y_mult_done), ("s", ln4_sq_s)], inc="dgu")
            for j in range(16):
                w = [("dgu", dgu), ("s", max(silu_free, ln3_s_done, ln4_sq_s)),
                     ("v", fuse_written)]
                P.emit("tensor", lambda e, j=j, buf=buf, hf=hf: e.matmul(
                    psB.ap()[:, 0:512], wgu_sb[buf].ap()[:, j * 128:(j + 1) * 128],
                    XT3r[:, j, hf * 512:(hf + 1) * 512],
                    start=(j == 0), stop=(j == 15)), waits=w if j == 0 else ())
                P.emit("tensor", lambda e, j=j, buf=buf, hf=hf: e.matmul(
                    psD.ap()[:, 0:512], wgu_sb[buf].ap()[:, 2048 + j * 128:2048 + (j + 1) * 128],
                    XT3r[:, j, hf * 512:(hf + 1) * 512],
                    start=(j == 0), stop=(j == 15)),
                    inc="t" if j == 15 else None)
            tgu = P.ctr["t"]
            fslice = fuse.ap()[:, c * 512:(c + 1) * 512]
            silu_free = P.emit("scalar", lambda e, c=c, fs=fslice: e.activation(
                fs, psB.ap()[:, 0:512], AF.Silu,
                bias=0.0, scale=sg_sb.ap()[:, c:c + 1]),
                waits=[("t", tgu), ("v", ln2_v_done), ("t", fuse_read)], inc="s")
            fuse_written = P.emit("vector", lambda e, c=c, fs=fslice:
                                  e.scalar_tensor_tensor(
                                      fs, psD.ap()[:, 0:512],
                                      su_sb.ap()[:, c:c + 1], fs,
                                      op0=ALU.mult, op1=ALU.mult),
                                  waits=[("s", silu_free)], inc="v")
            gu_free[buf] = P.ctr["t"]
        fuse_v_done = P.ctr["v"]

        y_bufs = (y_buf, y_buf2)
        for pr2 in range(2):
            tcs = (2 * pr2, 2 * pr2 + 1)
            ln4_of_pair = []
            for hg in range(4):
                banks = [t2i * 2 + hg % 2 for t2i in range(2)]
                for og in range(8):
                    buf = og % 2
                    dwd = P.emit("gpsimd", lambda e, hg=hg, og=og, buf=buf:
                                 e.dma_start(out=wd_sb[buf].ap(),
                                             in_=wd_t[hg, og]),
                                 waits=[("t", wd_free[buf])], inc="dwd")
                    for o8 in range(8):
                        c = og * 8 + o8
                        w = [("dwd", dwd), ("v", fuse_v_done),
                             ("v", y_mult_done),
                             ("s", max(ln4_sq_s, silu_free))]
                        for t2i, tc2 in enumerate(tcs):
                            psd = psA.ap()[:, banks[t2i] * 512:
                                           (banks[t2i] + 1) * 512]
                            P.emit("tensor",
                                   lambda e, c=c, o8=o8, tc2=tc2, buf=buf,
                                   psd=psd: e.matmul(
                                       psd,
                                       fuse.ap()[:, c * 512 + tc2 * 128:
                                                 c * 512 + (tc2 + 1) * 128],
                                       wd_sb[buf].ap()[:, o8 * 512:(o8 + 1) * 512],
                                       start=(c == 0), stop=(c == 63),
                                       skip_group_check=True),
                                   waits=w if (o8 == 0 and t2i == 0) else (),
                                   inc="t" if (o8 == 7 and t2i == 1) else None)
                    wd_free[buf] = P.ctr["t"]
                td = P.ctr["t"]
                for t2i in range(2):
                    psd = psA.ap()[:, banks[t2i] * 512:(banks[t2i] + 1) * 512]
                    P.emit("vector", lambda e, psd=psd, hg=hg, t2i=t2i:
                           e.tensor_mul(
                               y_bufs[t2i].ap()[:, hg * 512:(hg + 1) * 512], psd,
                               sdb.ap()[:, hg * 512:(hg + 1) * 512]),
                           waits=[("t", td), ("dwo", sdb_dw), ("s", ln4_sq_s)],
                           inc="v" if (hg == 3 and t2i == 1) else None)
            y_mult_done = P.ctr["v"]
            for t2i, tc2 in enumerate(tcs):
                slot = hf * 4 + tc2
                par = tc2 % 2
                o, ss, vd = ln_stats(y_bufs[t2i].ap(), psA.ap()[:, 0:DM], par,
                                     [("v", y_mult_done)])
                ln4_sq_s = P.ctr["s"]
                xr = x_res.ap()[:, slot * DM:(slot + 1) * DM]
                vstt4 = P.emit("vector", lambda e, xr=xr, o=o, t2i=t2i:
                    e.scalar_tensor_tensor(
                        xr, y_bufs[t2i].ap(),
                        stats.ap()[:, C_RSTD + o:C_RSTD + o + 1], xr,
                        op0=ALU.mult, op1=ALU.add),
                    waits=[("v", vd), ("s", ss), ("v", ln3_v_done)], inc="v")
                vfin = P.emit("vector", lambda e, xr=xr, o=o: e.tensor_scalar(
                    xr, xr, stats.ap()[:, C_NMR + o:C_NMR + o + 1], None,
                    op0=ALU.add), waits=[("v", vstt4)], inc="v")
                ln4_v_done = vfin
                out_dx.append(P.emit("sync", lambda e, slot=slot: e.dma_start(
                    out=y_d[slot * 128:(slot + 1) * 128, :],
                    in_=x_res.ap()[:, slot * DM:(slot + 1) * DM]),
                    waits=[("v", vfin)], inc="dout"))
            y_mult_done = P.ctr["v"]
        fuse_read = P.ctr["t"]

    dx_final = P.ctr["dout"]
    P.emit("sync", lambda e, v=dx_final: e.wait_ge(sems["dout"], v))

    return _finish()


def _own_rows(r):
    return np.concatenate([np.arange((2 * i + r) * 128, (2 * i + r + 1) * 128)
                           for i in range(NOWN)])


def _rope_tables():
    inv = 1.0 / (10000.0 ** (np.arange(0, HD, 2, dtype=np.float64) / HD))
    ang = np.concatenate([inv, inv])[:, None] * np.arange(S, dtype=np.float64)[None, :]
    cosf = np.cos(ang)
    sinf = np.sin(ang)
    sin2 = sinf.copy()
    sin2[0:64] *= -1.0
    sin2 = np.roll(sin2, 64, axis=0)
    return cosf, sin2


def _masks(r):
    m = np.zeros((NOWN, 128, 256), np.float32)
    tk = np.arange(256)
    p = np.arange(128)
    for i in range(NOWN):
        q = 2 * i + r
        gtq = q * 128 + p
        gtk = 2 * i * 128 + tk
        m[i] = np.where(gtk[None, :] <= gtq[:, None], 0.0, -1e30)
    return np.ascontiguousarray(m.transpose(1, 0, 2).reshape(128, NOWN * 256))


def kernel(L0_attn_x, L0_attn_Wq, L0_attn_Wk, L0_attn_Wv, L0_attn_Wo,
           L0_ffn_w_gate, L0_ffn_w_up, L0_ffn_w_down,
           L0_ffn_s_gate, L0_ffn_s_up, L0_ffn_s_down):
    x = np.ascontiguousarray(np.asarray(L0_attn_x, dtype=np.float32))
    Wq = np.asarray(L0_attn_Wq, dtype=np.float32)
    Wk = np.asarray(L0_attn_Wk, dtype=np.float32)
    Wv = np.asarray(L0_attn_Wv, dtype=np.float32)
    Wo = np.asarray(L0_attn_Wo, dtype=np.float32)
    wg = np.asarray(L0_ffn_w_gate).astype(np.int8)
    wu = np.asarray(L0_ffn_w_up).astype(np.int8)
    wd = np.asarray(L0_ffn_w_down).astype(np.int8)
    sg = np.asarray(L0_ffn_s_gate, dtype=np.float32)
    su = np.asarray(L0_ffn_s_up, dtype=np.float32)
    sd = np.asarray(L0_ffn_s_down, dtype=np.float32)

    def qkv_tile(W):
        return np.ascontiguousarray(
            W.reshape(H, HD, 16, 128).transpose(0, 3, 2, 1))

    wq_t = qkv_tile(Wq)
    wk_t = np.ascontiguousarray(
        qkv_tile(Wk).reshape(8, 2, 128, 16, 128).transpose(0, 2, 1, 3, 4)
        .reshape(8, 128, 4096))
    wv_t = np.ascontiguousarray(
        qkv_tile(Wv).reshape(8, 2, 128, 16, 128).transpose(0, 2, 3, 1, 4)
        .reshape(8, 128, 4096))
    wo_t = np.ascontiguousarray(Wo.reshape(4, 512, 16, 128).transpose(0, 3, 2, 1))
    wg_t = wg.reshape(16, 128, 64, 128).transpose(2, 1, 0, 3).reshape(64, 128, 2048)
    wu_t = wu.reshape(16, 128, 64, 128).transpose(2, 1, 0, 3).reshape(64, 128, 2048)
    wgu_t = np.ascontiguousarray(np.concatenate([wg_t, wu_t], axis=2))
    wd_t = np.ascontiguousarray(
        wd.reshape(8, 8, 128, 4, 512).transpose(3, 0, 2, 1, 4)
        .reshape(4, 8, 128, 4096))
    sg_t = np.ascontiguousarray(sg.reshape(64, 128).T)
    su_t = np.ascontiguousarray(su.reshape(64, 128).T)
    sdb = np.ascontiguousarray(np.broadcast_to(sd[None, :], (128, DM)))

    cosf, sin2 = _rope_tables()
    cosk = np.ascontiguousarray(cosf).astype(BF)
    sink2 = np.ascontiguousarray(sin2).astype(BF)
    identb = np.eye(128, dtype=BF)
    identf = np.eye(128, dtype=np.float32)
    ones = np.ones((1, 128), np.float32)

    nc = build_program()

    rows_by_role = [_own_rows(0), _own_rows(1)]
    role_data = []
    for r in (0, 1):
        rows = rows_by_role[r]
        role_data.append({
            "cosq": np.ascontiguousarray(cosf[:, rows] * ATT_SCALE).astype(BF),
            "sinq2": np.ascontiguousarray(sin2[:, rows] * ATT_SCALE).astype(BF),
            "maskq": _masks(r),
        })

    in_maps = []
    for c in range(8):
        b, r = c // 2, c % 2
        rows = rows_by_role[r]
        in_maps.append({
            "x_full": x[b],
            "x_own": np.ascontiguousarray(x[b][rows]),
            "wq_t": wq_t, "wk_t": wk_t, "wv_t": wv_t, "wo_t": wo_t,
            "wgu_t": wgu_t, "wd_t": wd_t,
            "sg": sg_t, "su": su_t, "sdb": sdb,
            "cosk": cosk, "sink2": sink2,
            "cosq": role_data[r]["cosq"], "sinq2": role_data[r]["sinq2"],
            "maskq": role_data[r]["maskq"],
            "identb": identb, "identf": identf, "ones": ones,
        })

    res = run_bass_kernel_spmd(nc, in_maps, core_ids=list(range(8)))

    out = np.empty((NB, S, DM), np.float32)
    for c in range(8):
        b, r = c // 2, c % 2
        out[b][rows_by_role[r]] = np.asarray(res.results[c]["y"])
    return out


# version 18
# speedup vs baseline: 1.1520x; 1.0043x over previous
import numpy as np
import ml_dtypes

import concourse.bass as bass
import concourse.mybir as mybir
from concourse.bass_utils import run_bass_kernel_spmd

F32 = mybir.dt.float32
BF16 = mybir.dt.bfloat16
I8 = mybir.dt.int8
AF = mybir.ActivationFunctionType
ALU = mybir.AluOpType
AX = mybir.AxisListType
BF = ml_dtypes.bfloat16

S = 2048
DM = 2048
H = 16
HD = 128
INTER = 8192
NB = 4
NCH = S // 128
NOWN = 8
EPS = 1e-5
ATT_SCALE = HD ** -0.5

_SKIP_EVAC = [False]

C_SUM, C_NM, C_M2E, C_VPE, C_RC, C_RSTD, C_NMR, C_SUM2 = range(8)


class Prog:

    ENGINES = ("sync", "gpsimd", "tensor", "vector", "scalar")
    OWN = {"tensor": "t", "vector": "v", "scalar": "s"}
    DMA = ("dc", "dxa", "dxb", "dx2", "dout", "dwq", "dwk", "dwo", "dgu", "dwd", "dpc")

    def __init__(self, sems):
        self.sems = sems
        self.ctr = {k: 0 for k in sems}
        self.streams = {e: [] for e in self.ENGINES}

    def emit(self, engine, fn, waits=(), inc=None, amt=None):
        w = []
        for sname, v in waits:
            if v is None or v <= 0:
                continue
            assert v <= self.ctr[sname], (engine, sname, v, self.ctr[sname])
            if sname in self.DMA:
                assert v == self.ctr[sname], ("partial DMA wait", engine, sname,
                                              v, self.ctr[sname])
            w.append((sname, v))
        post = None
        incs = ()
        if inc is not None:
            if amt is None:
                amt = 16 if inc in self.DMA else 1
            self.ctr[inc] += amt
            post = self.ctr[inc]
            incs = ((inc, amt),)
        self.streams[engine].append((tuple(w), fn, incs))
        return post

    def replay(self, engine, eng):
        last = {}
        for waits, fn, incs in self.streams[engine]:
            for sname, v in waits:
                if last.get(sname, 0) >= v:
                    continue
                eng.wait_ge(self.sems[sname], v)
                last[sname] = v
            inst = fn(eng)
            for sname, amt in incs:
                inst.then_inc(self.sems[sname], amt)


def build_program(limit=None, taps=(), p3_groups=8, p3_pipe=True):
    nc = bass.Bass("TRN2", target_bir_lowering=False,
                   detect_race_conditions=False)

    x_full = nc.dram_tensor("x_full", [S, DM], F32, kind="ExternalInput").ap()
    x_own = nc.dram_tensor("x_own", [NOWN * 128, DM], F32, kind="ExternalInput").ap()
    wq_t = nc.dram_tensor("wq_t", [H, 128, 16, 128], F32, kind="ExternalInput").ap()
    wk_t = nc.dram_tensor("wk_t", [8, 128, 4096], F32, kind="ExternalInput").ap()
    wv_t = nc.dram_tensor("wv_t", [8, 128, 4096], F32, kind="ExternalInput").ap()
    wo_t = nc.dram_tensor("wo_t", [4, 128, 16, 512], F32, kind="ExternalInput").ap()
    wgu_t = nc.dram_tensor("wgu_t", [64, 128, 4096], I8, kind="ExternalInput").ap()
    wd_t = nc.dram_tensor("wd_t", [4, 8, 128, 4096], I8, kind="ExternalInput").ap()
    sg_d = nc.dram_tensor("sg", [128, 64], F32, kind="ExternalInput").ap()
    su_d = nc.dram_tensor("su", [128, 64], F32, kind="ExternalInput").ap()
    sdb_d = nc.dram_tensor("sdb", [128, DM], F32, kind="ExternalInput").ap()
    cosk_d = nc.dram_tensor("cosk", [128, S], BF16, kind="ExternalInput").ap()
    sink_d = nc.dram_tensor("sink2", [128, S], BF16, kind="ExternalInput").ap()
    cosq_d = nc.dram_tensor("cosq", [128, 1024], BF16, kind="ExternalInput").ap()
    sinq_d = nc.dram_tensor("sinq2", [128, 1024], BF16, kind="ExternalInput").ap()
    mask_d = nc.dram_tensor("maskq", [128, NOWN * 256], F32, kind="ExternalInput").ap()
    idb_d = nc.dram_tensor("identb", [128, 128], BF16, kind="ExternalInput").ap()
    idf_d = nc.dram_tensor("identf", [128, 128], F32, kind="ExternalInput").ap()
    ones_d = nc.dram_tensor("ones", [1, 128], F32, kind="ExternalInput").ap()
    nsh_d = nc.dram_tensor("negshift", [128, 1], F32, kind="ExternalInput").ap()
    y_d = nc.dram_tensor("y", [NOWN * 128, DM], F32, kind="ExternalOutput").ap()

    SBASE = 16512
    A0, B0, C0, D0, E0 = (SBASE, SBASE + 65536, SBASE + 98304,
                          SBASE + 131072, SBASE + 196608)

    def at(name, shape, dtype, off):
        return nc.alloc_sbuf_tensor_at(name, shape, dtype, offset=off)

    XT1 = at("XT1", [128, 16 * S], BF16, A0)
    o_full = at("o_full", [128, NOWN * DM], F32, A0)
    fuse = at("fuse", [128, 64 * 512], BF16, A0)

    XTQ = at("XTQ", [128, 16 * 1024], BF16, B0)
    mask_sb = at("mask_sb", [128, NOWN * 256], F32, B0)
    cosk = at("cosk_sb", [128, S], BF16, B0 + 8192)
    sink2 = at("sink_sb", [128, S], BF16, B0 + 12288)
    wk2 = at("wk2", [128, 2 * 16 * 128], BF16, B0 + 16384)
    wv2 = at("wv2", [128, 16 * 256], BF16, B0 + 24576)
    XT3 = at("XT3", [128, 16 * 1024], BF16, B0)

    qall = at("qall", [128, H * 1024], BF16, C0)
    wo_sb = [at(f"wo{i}", [128, 16 * 512], BF16, C0 + i * 16384) for i in (0, 1)]
    wgu_sb = [at(f"wgu{i}", [128, 2 * 16 * 128], BF16, C0 + i * 8192) for i in (0, 1)]
    wd_sb = [at(f"wd{i}", [128, 8 * 512], BF16, C0 + 16384 + i * 8192) for i in (0, 1)]

    xstage = at("xstage", [128, 2 * DM], F32, D0)
    xn = at("xn", [128, 2 * DM], BF16, D0 + 16384)
    wq_sb = [at(f"wq{i}", [128, 16 * 128], BF16, D0 + 24576 + i * 4096) for i in (0, 1)]
    v2_sb = at("v2", [128, 16 * 256], BF16, D0)
    v2_sb2 = at("v2b", [128, 16 * 256], BF16, D0 + 57344)
    k_h = at("k_h", [128, S], BF16, D0 + 8192)
    PT_sb = at("PT", [128, 2 * 1024], BF16, D0 + 12288)
    qsb = at("qsb", [128, 512], BF16, D0 + 32768)
    ksb = at("ksb", [128, 512], BF16, D0 + 56320)
    rope_tmp = at("rope_tmp", [128, 512], BF16, D0 + 53248)
    AVALL = at("AVALL", [128, H * 1024], BF16, D0 + 20480)
    x_res = at("x_res", [128, NOWN * DM], F32, D0)

    p_row = at("p_row", [128, S], BF16, E0)
    p_row2 = at("p_row2", [128, S], BF16, E0 + 4096)
    cosq = at("cosq_sb", [128, 1024], BF16, E0 + 4096)
    sinq2 = at("sinq_sb", [128, 1024], BF16, E0 + 6144)
    xstage2 = at("xstage2", [128, DM], F32, E0)
    xn3 = at("xn3", [128, DM], BF16, E0)
    y_buf = at("y_buf", [128, DM], F32, E0)
    bc_sb = at("bc_sb", [128, 512], F32, E0 + 8192)
    sdb = at("sdb_sb", [128, DM], BF16, E0 + 8192)
    identb = at("identb_sb", [128, 128], BF16, E0 + 12288)
    identf = at("identf_sb", [128, 128], F32, E0 + 12544)
    ones_sb = at("ones_sb", [1, 128], F32, E0 + 13056)
    sg_sb = at("sg_sb", [128, 64], F32, E0 + 13568)
    su_sb = at("su_sb", [128, 64], F32, E0 + 13824)
    stats = at("stats", [128, 16], F32, E0 + 14080)
    sums8 = at("sums8", [128, 8], F32, E0 + 14144)
    rcp8 = at("rcp8", [128, 8], F32, E0 + 14176)
    nmcol = at("nmcol", [128, 1], F32, E0 + 14208)
    negshift = at("negshift", [128, 1], F32, E0 + 14240)
    rrow = at("rrow", [1, 512], F32, D0 + 54272)
    y_buf2 = at("y_buf2", [128, DM], F32, C0)

    psA = nc.alloc_psum_tensor("psA", [128, 2048], F32)
    psB = nc.alloc_psum_tensor("psB", [128, 512], F32)
    psC = nc.alloc_psum_tensor("psC", [128, 2048], BF16)
    psD = nc.alloc_psum_tensor("psD", [128, 512], F32)

    from contextlib import ExitStack
    ctx = ExitStack()
    sems = {}
    for sname in Prog.DMA + ("t", "v", "s"):
        sems[sname] = ctx.enter_context(nc.semaphore("sem_" + sname))
    P = Prog(sems)

    def _finish():
        for tname in taps:
            th = tapreg[tname]
            shape = list(th.shape)
            td = nc.dram_tensor("tap_" + tname, shape, th.dtype,
                                kind="ExternalOutput").ap()
            P.emit("sync", lambda e, th=th, td=td: e.dma_start(
                out=td, in_=th.ap()),
                waits=[("t", P.ctr["t"]), ("v", P.ctr["v"]), ("s", P.ctr["s"])],
                inc="dout")
        for eng, semlist in (("sync", ("dc", "dxa", "dxb", "dx2", "dout")),
                             ("gpsimd", ("dwq", "dwk", "dwo", "dgu", "dwd"))):
            for sname in semlist:
                v = P.ctr[sname]
                if v:
                    P.emit(eng, lambda e, s=sname, v=v: e.wait_ge(sems[s], v))
        with nc.Block() as block:
            block.sync(lambda e: P.replay("sync", e))
            block.gpsimd(lambda e: P.replay("gpsimd", e))
            block.tensor(lambda e: P.replay("tensor", e))
            block.vector(lambda e: P.replay("vector", e))
            block.scalar(lambda e: P.replay("scalar", e))
        ctx.close()
        return nc


    XT1r = XT1.ap().rearrange("p (j t) -> p j t", j=16)
    XTQr = XTQ.ap().rearrange("p (j t) -> p j t", j=16)
    XT3r = XT3.ap().rearrange("p (j t) -> p j t", j=16)
    psCr = psC.ap().rearrange("p (n t) -> p n t", n=16)

    def ln_stats(src_ap, sq_out_ap, par, first_waits):
        o = 8 * par
        st = stats.ap()
        v0 = P.emit("vector", lambda e, a=src_ap, o=o: e.reduce_sum(
            st[:, o + C_SUM:o + C_SUM + 1], a, axis=AX.X), waits=first_waits,
            inc="v")
        sq = P.emit("scalar", lambda e, a=src_ap, b=sq_out_ap, o=o: e.activation(
            b, a, AF.Square, bias=0.0, scale=1.0,
            accum_out=st[:, o + C_SUM2:o + C_SUM2 + 1]),
            waits=first_waits, inc="s")
        v1 = P.emit("vector", lambda e, o=o: e.tensor_scalar(
            st[:, o + C_NM:o + C_NM + 1], st[:, o + C_SUM:o + C_SUM + 1],
            -1.0 / DM, None, op0=ALU.mult), waits=[("v", v0)], inc="v")
        v2 = P.emit("vector", lambda e, o=o: e.tensor_scalar(
            st[:, o + C_M2E:o + C_M2E + 1], st[:, o + C_NM:o + C_NM + 1],
            st[:, o + C_NM:o + C_NM + 1], EPS, op0=ALU.mult, op1=ALU.subtract),
            waits=[("v", v1)], inc="v")
        v3 = P.emit("vector", lambda e, o=o: e.tensor_scalar(
            st[:, o + C_VPE:o + C_VPE + 1], st[:, o + C_SUM2:o + C_SUM2 + 1],
            1.0 / DM, st[:, o + C_M2E:o + C_M2E + 1],
            op0=ALU.mult, op1=ALU.subtract),
            waits=[("s", sq), ("v", v2)], inc="v")
        vv = P.emit("vector", lambda e, o=o: e.reciprocal(
            st[:, o + C_RC:o + C_RC + 1], st[:, o + C_VPE:o + C_VPE + 1]),
            waits=[("v", v3)], inc="v")
        ss = P.emit("scalar", lambda e, o=o: e.activation(
            st[:, o + C_RSTD:o + C_RSTD + 1], st[:, o + C_RC:o + C_RC + 1],
            AF.Sqrt), waits=[("v", vv)], inc="s")
        vd = P.emit("vector", lambda e, o=o: e.tensor_mul(
            st[:, o + C_NMR:o + C_NMR + 1], st[:, o + C_NM:o + C_NM + 1],
            st[:, o + C_RSTD:o + C_RSTD + 1]), waits=[("s", ss)], inc="v")
        return o, ss, vd

    psC_evac = {0: 0, 1: 0}

    def transpose_batch(src_tiles, dst_ap, half, t_waits=(), evac_waits=(),
                        identity=None):
        idt = identity if identity is not None else identb
        n = len(src_tiles)
        tw = list(t_waits)
        if psC_evac[half]:
            tw.append(("v", psC_evac[half]))
        tt = None
        for k, sap in enumerate(src_tiles):
            tt = P.emit("tensor", lambda e, a=sap, k=k, half=half: e.transpose(
                psC.ap()[:, half * 1024 + k * 128: half * 1024 + (k + 1) * 128],
                a, idt.ap()[:, :]), waits=tw if k == 0 else (),
                inc="t" if k == n - 1 else None)
        if _SKIP_EVAC[0]:
            ve = P.emit("vector", lambda e: e.nop(),
                        waits=[("t", tt)] + list(evac_waits), inc="v")
            psC_evac[half] = ve
            return ve
        src = psCr[:, half * 8: half * 8 + n, :]
        ve = P.emit("vector", lambda e, s=src, d=dst_ap: e.tensor_copy(d, s),
                    waits=[("t", tt)] + list(evac_waits), inc="v")
        psC_evac[half] = ve
        return ve

    tapreg = {"XT1": XT1, "XTQ": XTQ, "stats": stats, "qall": qall,
              "k_h": k_h, "v2": v2_sb, "AVALL": AVALL, "o_full": o_full,
              "x_res": x_res, "fuse": fuse, "xn": xn, "p_row": p_row,
              "sums8": sums8, "rcp8": rcp8, "mask_sb": mask_sb, "cosk": cosk,
              "xstage": xstage, "y_buf": y_buf, "XT3": XT3}

    def fence():
        pass

    P.emit("sync", lambda e: e.dma_start(out=negshift.ap(), in_=nsh_d), inc="dc")
    P.emit("sync", lambda e: e.dma_start(out=identb.ap(), in_=idb_d), inc="dc")
    P.emit("sync", lambda e: e.dma_start(out=identf.ap(), in_=idf_d), inc="dc")
    P.emit("sync", lambda e: e.dma_start(out=ones_sb.ap(), in_=ones_d), inc="dc")
    P.emit("sync", lambda e: e.dma_start(out=cosq.ap(), in_=cosq_d), inc="dc")
    P.emit("sync", lambda e: e.dma_start(out=sinq2.ap(), in_=sinq_d), inc="dc")
    P.emit("sync", lambda e: e.dma_start(out=sg_sb.ap(), in_=sg_d), inc="dc")
    dx_const = P.emit("sync", lambda e: e.dma_start(out=su_sb.ap(), in_=su_d),
                      inc="dc")

    norm_done = {}
    trans_t = {}

    def ln_chunk(ci, kind, c):
        par = ci % 2
        src_dram = (x_full if kind == "full" else x_own)[c * 128:(c + 1) * 128, :]
        xs = xstage.ap()[:, par * DM:(par + 1) * DM]
        xnp = xn.ap()[:, par * DM:(par + 1) * DM]
        dsem = "dxa" if par == 0 else "dxb"
        dxc = P.emit("sync", lambda e, a=xs, b=src_dram: e.dma_start(out=a, in_=b),
                     waits=[("s", norm_done.get(ci - 2, 0))], inc=dsem)
        fw = [(dsem, dxc), ("t", trans_t.get(ci - 2, 0))]
        o, ss, vd = ln_stats(xs, xnp, par, fw)
        nd = P.emit("scalar", lambda e, a=xnp, b=xs, o=o: e.activation(
            a, b, AF.Identity,
            bias=stats.ap()[:, o + C_NMR:o + C_NMR + 1],
            scale=stats.ap()[:, o + C_RSTD:o + C_RSTD + 1]),
            waits=[("v", vd)], inc="s")
        norm_done[ci] = nd
        for bt in range(4):
            tiles = [xnp[:, (bt * 4 + k) * 128:(bt * 4 + k + 1) * 128]
                     for k in range(4)]
            r = XT1r if kind == "full" else XTQr
            dst = r[:, bt * 4:bt * 4 + 4, c * 128:(c + 1) * 128]
            transpose_batch(tiles, dst, half=bt % 2, t_waits=[("s", nd)])
        trans_t[ci] = P.ctr["t"]

    for c in range(NOWN):
        ln_chunk(c, "own", c)
    xtq_v_done = P.ctr["v"]

    q_mm_done = {}
    qsb_free = 0
    psQ_free = {0: 0, 1: 0}

    def q_head(h):
        global_unused = None
        wbuf = wq_sb[h % 2]
        w = [("t", q_mm_done.get(h - 2, 0)), ("s", norm_done[NOWN - 1])]
        dwc = P.emit("gpsimd", lambda e, a=wbuf, b=wq_t[h]: e.dma_start(
            out=a.ap(), in_=b), waits=w, inc="dwq")
        nonlocal_state = None
        for G in range(2):
            ps = (psB if G == 0 else psD).ap()[:, 0:512]
            for j in range(16):
                w = [("dwq", dwc), ("s", psQ_free[G]), ("v", xtq_v_done)]
                P.emit("tensor", lambda e, ps=ps, a=wbuf, j=j, G=G: e.matmul(
                    ps, a.ap()[:, j * 128:(j + 1) * 128],
                    XTQr[:, j, G * 512:(G + 1) * 512],
                    start=(j == 0), stop=(j == 15)),
                    waits=w if j == 0 else (), inc="t" if j == 15 else None)
            tq = P.ctr["t"]
            ev = P.emit("scalar", lambda e, ps=ps: e.activation(
                qsb.ap(), ps, AF.Copy), waits=[("t", tq), ("v", q_head.qsb_free)],
                inc="s")
            psQ_free[G] = ev
            qa = qall.ap()[:, h * 1024 + G * 512: h * 1024 + (G + 1) * 512]
            cq = cosq.ap()[:, G * 512:(G + 1) * 512]
            sq2 = sinq2.ap()[:, G * 512:(G + 1) * 512]
            P.emit("vector", lambda e, sq2=sq2: e.tensor_mul(
                rope_tmp.ap()[0:64, :], qsb.ap()[64:128, :], sq2[64:128, :]),
                waits=[("s", ev), ("dc", dx_const)])
            P.emit("vector", lambda e, sq2=sq2: e.tensor_mul(
                rope_tmp.ap()[64:128, :], qsb.ap()[0:64, :], sq2[0:64, :]))
            q_head.qsb_free = P.emit("vector", lambda e, qa=qa, cq=cq: e.tensor_mul(
                qa, qsb.ap(), cq), inc="v")
            P.emit("vector", lambda e, qa=qa: e.tensor_add(
                qa, qa, rope_tmp.ap()), waits=[("v", q_head.qsb_free)], inc="v")
        q_mm_done[h] = P.ctr["t"]
    q_head.qsb_free = 0

    for h in range(H):
        ln_chunk(NOWN + h, "full", h)
        q_head(h)
    ln1_t_done = P.ctr["t"]
    ln1_v_done = P.ctr["v"]
    ln1_s_done = P.ctr["s"]
    if limit == 1:
        return _finish()
    fence()
    p2_t_done = P.ctr["t"]
    p2_s_done = P.ctr["s"]
    p2_v_done = P.ctr["v"]
    if limit == 2:
        return _finish()
    w = [("t", p2_t_done)]
    P.emit("sync", lambda e: e.dma_start(out=mask_sb.ap(), in_=mask_d), waits=w,
           inc="dc")
    P.emit("sync", lambda e: e.dma_start(out=cosk.ap(), in_=cosk_d), inc="dc")
    dx_k = P.emit("sync", lambda e: e.dma_start(out=sink2.ap(), in_=sink_d),
                  inc="dc")
    mask_r = mask_sb.ap().rearrange("p (i c) -> p i c", i=NOWN)

    kv_buf_free = p2_t_done
    v2_free = 0
    k_h_free = 0
    psA_free_s = p2_s_done
    ksb_free = 0
    exp_prev = 0

    class att:
        av_read = 0
        pt_free = {0: 0, 1: 0}
        pt_slot_done = {0: 0, 1: 0}
        psD_free_v = 0
        sums_free = {k: 0 for k in range(8)}

    def vbuf(g2):
        return v2_sb if g2 % 2 == 0 else v2_sb2

    def emit_kv_dma(g2, wk_free_t, v2_free_t):
        P.emit("gpsimd", lambda e, g2=g2: e.dma_start(
            out=wk2.ap(), in_=wk_t[g2]),
            waits=[("t", wk_free_t)], inc="dwk")
        return P.emit("gpsimd", lambda e, g2=g2: e.dma_start(
            out=wv2.ap(), in_=wv_t[g2]),
            waits=[("t", v2_free_t)], inc="dwk")

    def emit_vprod_tc(g2, tc, dwk_val):
        ps = psD.ap()[:, 0:256]
        for j in range(16):
            w = [("dwk", dwk_val), ("v", max(att.vprod_evac, ln1_v_done))]
            P.emit("tensor", lambda e, ps=ps, j=j, tc=tc: e.matmul(
                ps, XT1r[:, j, tc * 128:(tc + 1) * 128],
                wv2.ap()[:, j * 256:(j + 1) * 256],
                start=(j == 0), stop=(j == 15)),
                waits=w if j == 0 else (), inc="t" if j == 15 else None)
        tvp = P.ctr["t"]
        att.vprod_evac = P.emit("vector", lambda e, ps=ps, tc=tc, g2=g2:
            e.tensor_copy(
                vbuf(g2).ap()[:, tc * 256:(tc + 1) * 256], ps),
            waits=[("t", tvp)], inc="v")

    att.vprod_evac = p2_v_done
    dwk_by_group = {0: None}
    vprod_done_by_group = {}
    v2_free_par = {0: 0, 1: 0}

    for g2 in range(p3_groups):
        if g2 == 0:
            dwk_by_group[0] = emit_kv_dma(0, kv_buf_free, 0)
            for tc in range(NCH):
                emit_vprod_tc(0, tc, dwk_by_group[0])
            vprod_done_by_group[0] = P.ctr["v"]
        vprod_done_s = vprod_done_by_group[g2]

        for hh in range(2):
            h = 2 * g2 + hh
            krope_g = {}
            for g in range(4):
                ps = psA.ap()[:, g * 512:(g + 1) * 512]
                for j in range(16):
                    w = [("dwk", dwk_by_group[g2]), ("s", psA_free_s),
                         ("v", ln1_v_done), ("s", ln1_s_done)]
                    P.emit("tensor", lambda e, ps=ps, j=j, hh=hh, g=g: e.matmul(
                        ps,
                        wk2.ap()[:, hh * 2048 + j * 128: hh * 2048 + (j + 1) * 128],
                        XT1r[:, j, g * 512:(g + 1) * 512],
                        start=(j == 0), stop=(j == 15)),
                        waits=w if j == 0 else (), inc="t" if j == 15 else None)
                tk = P.ctr["t"]
                ev = P.emit("scalar", lambda e, ps=ps: e.activation(
                    ksb.ap(), ps, AF.Copy),
                    waits=[("t", tk), ("v", ksb_free)], inc="s")
                kh = k_h.ap()[:, g * 512:(g + 1) * 512]
                ck = cosk.ap()[:, g * 512:(g + 1) * 512]
                sk = sink2.ap()[:, g * 512:(g + 1) * 512]
                P.emit("vector", lambda e, sk=sk: e.tensor_mul(
                    rope_tmp.ap()[0:64, :], ksb.ap()[64:128, :], sk[64:128, :]),
                    waits=[("s", ev), ("dc", dx_k), ("t", k_h_free)])
                P.emit("vector", lambda e, sk=sk: e.tensor_mul(
                    rope_tmp.ap()[64:128, :], ksb.ap()[0:64, :], sk[0:64, :]))
                ksb_free = P.emit("vector", lambda e, kh=kh, ck=ck: e.tensor_mul(
                    kh, ksb.ap(), ck), inc="v")
                krope_g[g] = P.emit("vector", lambda e, kh=kh: e.tensor_add(
                    kh, kh, rope_tmp.ap()), waits=[("v", ksb_free)], inc="v")
            krope_v = P.ctr["v"]
            psA_scores_wait = P.ctr["s"]
            if hh == 1 and g2 + 1 < p3_groups:
                dwk_by_group[g2 + 1] = emit_kv_dma(
                    g2 + 1, P.ctr["t"], v2_free_par[(g2 + 1) % 2])

            exp_done = {}
            evac_done = {}
            sc_done = {}

            def emit_pt_av(i):
                nt = 2 * i + 2
                G = i // 4
                ed = exp_done[i]
                pr = (p_row if i % 2 == 0 else p_row2).ap()
                batches = [list(range(cb * 8, min(cb * 8 + 8, nt)))
                           for cb in range((nt + 7) // 8)]
                ves = []
                for cb, cs in enumerate(batches):
                    half = cb % 2
                    tiles = [pr[:, c * 128:(c + 1) * 128] for c in cs]
                    dst = PT_sb.ap()[:, half * 1024: half * 1024 + len(cs) * 128]
                    ves.append(transpose_batch(
                        tiles, dst, half=half, t_waits=[("v", sc_done[i])],
                        evac_waits=[("t", att.pt_free[half])]))
                for cb, cs in enumerate(batches):
                    half = cb % 2
                    for c in cs:
                        w = [("v", max(ves[cb], vprod_done_s)),
                             ("v", att.av_read)]
                        P.emit("tensor",
                               lambda e, c=c, cb=cb, i=i, half=half, nt=nt,
                               hh=hh, g2=g2: e.matmul(
                                   psB.ap()[:, (i % 4) * 128:(i % 4 + 1) * 128],
                                   vbuf(g2).ap()[:, c * 256 + hh * 128:
                                                 c * 256 + (hh + 1) * 128],
                                   PT_sb.ap()[:, half * 1024 + (c - cb * 8) * 128:
                                              half * 1024 + (c - cb * 8 + 1) * 128],
                                   start=(i % 4 == 0 and c == 0),
                                   stop=(i % 4 == 3 and c == nt - 1),
                                   skip_group_check=True),
                               waits=w if c == cs[0] else (),
                               inc="t" if c == cs[-1] else None)
                    att.pt_free[half] = P.ctr["t"]
                att.pt_slot_done[i % 2] = P.ctr["t"]
                if i % 4 == 3:
                    av_t = P.ctr["t"]
                    att.av_read = P.emit("vector", lambda e, G=G, h=h:
                        e.tensor_copy(
                            AVALL.ap()[:, h * 1024 + G * 512:
                                       h * 1024 + (G + 1) * 512],
                            psB.ap()[:, 0:512]),
                        waits=[("t", av_t)], inc="v")

            for i in range(NOWN):
                W = (2 * i + 2) * 128
                ng = (W + 511) // 512
                for g in range(ng):
                    n = min(512, W - g * 512)
                    w = [("v", krope_g[ng - 1]),
                         ("s", evac_done.get(i - 1, psA_scores_wait))]
                    P.emit("tensor", lambda e, g=g, n=n, h=h, i=i: e.matmul(
                        psA.ap()[:, g * 512:g * 512 + n],
                        qall.ap()[:, h * 1024 + i * 128: h * 1024 + (i + 1) * 128],
                        k_h.ap()[:, g * 512:g * 512 + n],
                        start=True, stop=True),
                        waits=w if g == 0 else (), inc="t" if g == ng - 1 else None)
                tsc = P.ctr["t"]
                if p3_pipe and i > 0:
                    emit_pt_av(i - 1)
                vmask = P.emit("vector", lambda e, W=W, i=i: e.tensor_add(
                    psA.ap()[:, W - 256:W], psA.ap()[:, W - 256:W],
                    mask_r[:, i, :]), waits=[("t", tsc)], inc="v")
                pr = (p_row if i % 2 == 0 else p_row2).ap()
                ed = P.emit("scalar", lambda e, W=W, i=i, pr=pr:
                    e.activation(
                        pr[:, 0:W], psA.ap()[:, 0:W], AF.Exp,
                        bias=negshift.ap()[:, 0:1], scale=1.0,
                        accum_out=sums8.ap()[:, i % 8:i % 8 + 1]),
                    waits=[("v", vmask), ("t", att.pt_slot_done[i % 2]),
                           ("v", att.sums_free[i % 8])], inc="s")
                exp_done[i] = ed
                evac_done[i] = ed
                exp_prev = ed
                rc = P.emit("vector", lambda e, i=i: e.reciprocal(
                    rcp8.ap()[:, i % 8:i % 8 + 1],
                    sums8.ap()[:, i % 8:i % 8 + 1]), waits=[("s", ed)], inc="v")
                att.sums_free[i % 8] = rc
                sc_done[i] = P.emit("vector", lambda e, i=i, pr=pr, W=W:
                    e.tensor_scalar(
                        pr[:, 0:W], pr[:, 0:W],
                        rcp8.ap()[:, i % 8:i % 8 + 1], None, op0=ALU.mult),
                    waits=[("v", rc)], inc="v")
                if hh == 1 and g2 + 1 < p3_groups:
                    emit_vprod_tc(g2 + 1, 2 * i, dwk_by_group[g2 + 1])
                    emit_vprod_tc(g2 + 1, 2 * i + 1, dwk_by_group[g2 + 1])
                if not p3_pipe:
                    emit_pt_av(i)
            if p3_pipe:
                emit_pt_av(NOWN - 1)
            psA_free_s = evac_done[NOWN - 1]
            k_h_free = P.ctr["t"]
            if hh == 1 and g2 + 1 < p3_groups:
                vprod_done_by_group[g2 + 1] = P.ctr["v"]
        v2_free_par[g2 % 2] = P.ctr["t"]
        kv_buf_free = P.ctr["t"]
    fence()
    attn_v_done = P.ctr["v"]
    attn_t_done = P.ctr["t"]
    attn_s_done = P.ctr["s"]
    if limit == 3:
        return _finish()

    wo_free = {0: attn_t_done, 1: attn_t_done}
    oproj_evac = {}
    for eg in range(4):
        dwo = P.emit("gpsimd", lambda e, eg=eg: e.dma_start(
            out=wo_sb[eg % 2].ap(), in_=wo_t[eg]),
            waits=[("t", wo_free[eg % 2])], inc="dwo")
        for tc in range(NOWN):
            bank = tc % 4
            ps = psA.ap()[:, bank * 512:(bank + 1) * 512]
            for j in range(16):
                w = [("dwo", dwo), ("s", max(oproj_evac.get((eg, tc - 4), 0),
                                             psA_free_s, attn_s_done)),
                     ("v", attn_v_done)]
                P.emit("tensor", lambda e, ps=ps, j=j, tc=tc, eg=eg: e.matmul(
                    ps,
                    AVALL.ap()[:, j * 1024 + tc * 128: j * 1024 + (tc + 1) * 128],
                    wo_sb[eg % 2].ap()[:, j * 512:(j + 1) * 512],
                    start=(j == 0), stop=(j == 15)),
                    waits=w if j == 0 else (), inc="t" if j == 15 else None)
            to = P.ctr["t"]
            oproj_evac[(eg, tc)] = P.emit(
                "scalar", lambda e, ps=ps, tc=tc, eg=eg: e.activation(
                    o_full.ap()[:, tc * DM + eg * 512: tc * DM + (eg + 1) * 512],
                    ps, AF.Copy), waits=[("t", to)], inc="s")
        wo_free[eg % 2] = P.ctr["t"]
    oproj_t_done = P.ctr["t"]
    oproj_s_done = P.ctr["s"]

    ln2_done = {}
    for tc in range(NOWN):
        par = tc % 2
        osrc = o_full.ap()[:, tc * DM:(tc + 1) * DM]
        dxc = P.emit("sync", lambda e, tc=tc: e.dma_start(
            out=xstage2.ap(), in_=x_own[tc * 128:(tc + 1) * 128, :]),
            waits=[("t", attn_t_done), ("v", ln2_done.get(tc - 1, 0))],
            inc="dx2")
        fw = [("s", oproj_evac[(3, tc)]), ("t", oproj_t_done)]
        o, ss, vd = ln_stats(osrc, psA.ap()[:, 0:DM], par, fw)
        xr = x_res.ap()[:, tc * DM:(tc + 1) * DM]
        vstt = P.emit("vector", lambda e, xr=xr, osrc=osrc, o=o:
            e.scalar_tensor_tensor(
                xr, osrc, stats.ap()[:, C_RSTD + o:C_RSTD + o + 1], xstage2.ap(),
                op0=ALU.mult, op1=ALU.add),
            waits=[("v", vd), ("s", ss), ("dx2", dxc), ("t", oproj_t_done)],
            inc="v")
        ln2_done[tc] = P.emit("vector", lambda e, xr=xr, o=o: e.tensor_scalar(
            xr, xr, stats.ap()[:, C_NMR + o:C_NMR + o + 1], None, op0=ALU.add),
            waits=[("v", vstt)], inc="v")
    fence()
    ln2_v_done = P.ctr["v"]
    if limit == 4:
        return _finish()

    ln3_tr_t = 0
    for tc in range(NOWN):
        par = tc % 2
        xr = x_res.ap()[:, tc * DM:(tc + 1) * DM]
        o, ss, vd = ln_stats(xr, psA.ap()[:, 0:DM], par,
                             [("v", ln2_done[tc]), ("t", oproj_t_done)])
        xnb = xn3.ap()[:, 0:DM]
        nd = P.emit("scalar", lambda e, xr=xr, o=o, xnb=xnb: e.activation(
            xnb, xr, AF.Identity,
            bias=stats.ap()[:, C_NMR + o:C_NMR + o + 1],
            scale=stats.ap()[:, C_RSTD + o:C_RSTD + o + 1]),
            waits=[("v", vd), ("t", ln3_tr_t)], inc="s")
        for bt in range(4):
            tiles = [xnb[:, (bt * 4 + k) * 128:(bt * 4 + k + 1) * 128]
                     for k in range(4)]
            dst = XT3r[:, bt * 4:bt * 4 + 4, tc * 128:(tc + 1) * 128]
            transpose_batch(tiles, dst, half=bt % 2, t_waits=[("s", nd)],
                            evac_waits=[("t", attn_t_done)])
        ln3_tr_t = P.ctr["t"]
    fence()
    ln3_t_done = P.ctr["t"]
    ln3_v_done = P.ctr["v"]
    ln3_s_done = P.ctr["s"]
    if limit == 5:
        return _finish()

    sdb_dw = P.emit("gpsimd", lambda e: e.dma_start(out=sdb.ap(), in_=sdb_d),
                    waits=[("v", attn_v_done)], inc="dwo")
    gu_free = {0: oproj_t_done, 1: oproj_t_done}
    fuse_read = 0
    silu_free = ln3_s_done
    wd_free = {0: oproj_t_done, 1: oproj_t_done}
    fuse_written = ln3_v_done
    fuse_written_init = ln3_v_done
    silu_par = {0: ln3_s_done, 1: ln3_s_done}
    fuse_par = {0: ln3_v_done, 1: ln3_v_done}
    y_mult_done = 0
    ln4_v_done = 0
    ln4_sq_s = ln3_s_done
    out_dx = []

    for hf in range(2):
        for c in range(64):
            buf = c % 2
            dgu = P.emit("gpsimd", lambda e, c=c, buf=buf: e.dma_start(
                out=wgu_sb[buf].ap(), in_=wgu_t[c]),
                waits=[("t", gu_free[buf]), ("t", fuse_read),
                       ("v", y_mult_done), ("s", ln4_sq_s)], inc="dgu")
            for j in range(16):
                w = [("dgu", dgu), ("s", max(silu_free, ln3_s_done, ln4_sq_s)),
                     ("v", fuse_written)]
                P.emit("tensor", lambda e, j=j, buf=buf, hf=hf: e.matmul(
                    psB.ap()[:, 0:512], wgu_sb[buf].ap()[:, j * 128:(j + 1) * 128],
                    XT3r[:, j, hf * 512:(hf + 1) * 512],
                    start=(j == 0), stop=(j == 15)), waits=w if j == 0 else ())
                P.emit("tensor", lambda e, j=j, buf=buf, hf=hf: e.matmul(
                    psD.ap()[:, 0:512], wgu_sb[buf].ap()[:, 2048 + j * 128:2048 + (j + 1) * 128],
                    XT3r[:, j, hf * 512:(hf + 1) * 512],
                    start=(j == 0), stop=(j == 15)),
                    inc="t" if j == 15 else None)
            tgu = P.ctr["t"]
            fslice = fuse.ap()[:, c * 512:(c + 1) * 512]
            silu_free = P.emit("scalar", lambda e, c=c, fs=fslice: e.activation(
                fs, psB.ap()[:, 0:512], AF.Silu,
                bias=0.0, scale=sg_sb.ap()[:, c:c + 1]),
                waits=[("t", tgu), ("v", ln2_v_done), ("t", fuse_read)], inc="s")
            fuse_written = P.emit("vector", lambda e, c=c, fs=fslice:
                                  e.scalar_tensor_tensor(
                                      fs, psD.ap()[:, 0:512],
                                      su_sb.ap()[:, c:c + 1], fs,
                                      op0=ALU.mult, op1=ALU.mult),
                                  waits=[("s", silu_free)], inc="v")
            gu_free[buf] = P.ctr["t"]
        fuse_v_done = P.ctr["v"]

        y_bufs = (y_buf, y_buf2)
        for pr2 in range(2):
            tcs = (2 * pr2, 2 * pr2 + 1)
            ln4_of_pair = []
            for hg in range(4):
                banks = [t2i * 2 + hg % 2 for t2i in range(2)]
                for og in range(8):
                    buf = og % 2
                    dwd = P.emit("gpsimd", lambda e, hg=hg, og=og, buf=buf:
                                 e.dma_start(out=wd_sb[buf].ap(),
                                             in_=wd_t[hg, og]),
                                 waits=[("t", wd_free[buf])], inc="dwd")
                    for o8 in range(8):
                        c = og * 8 + o8
                        w = [("dwd", dwd), ("v", fuse_v_done),
                             ("v", y_mult_done),
                             ("s", max(ln4_sq_s, silu_free))]
                        for t2i, tc2 in enumerate(tcs):
                            psd = psA.ap()[:, banks[t2i] * 512:
                                           (banks[t2i] + 1) * 512]
                            P.emit("tensor",
                                   lambda e, c=c, o8=o8, tc2=tc2, buf=buf,
                                   psd=psd: e.matmul(
                                       psd,
                                       fuse.ap()[:, c * 512 + tc2 * 128:
                                                 c * 512 + (tc2 + 1) * 128],
                                       wd_sb[buf].ap()[:, o8 * 512:(o8 + 1) * 512],
                                       start=(c == 0), stop=(c == 63),
                                       skip_group_check=True),
                                   waits=w if (o8 == 0 and t2i == 0) else (),
                                   inc="t" if (o8 == 7 and t2i == 1) else None)
                    wd_free[buf] = P.ctr["t"]
                td = P.ctr["t"]
                for t2i in range(2):
                    psd = psA.ap()[:, banks[t2i] * 512:(banks[t2i] + 1) * 512]
                    P.emit("vector", lambda e, psd=psd, hg=hg, t2i=t2i:
                           e.tensor_mul(
                               y_bufs[t2i].ap()[:, hg * 512:(hg + 1) * 512], psd,
                               sdb.ap()[:, hg * 512:(hg + 1) * 512]),
                           waits=[("t", td), ("dwo", sdb_dw), ("s", ln4_sq_s)],
                           inc="v" if (hg == 3 and t2i == 1) else None)
            y_mult_done = P.ctr["v"]
            for t2i, tc2 in enumerate(tcs):
                slot = hf * 4 + tc2
                par = tc2 % 2
                o, ss, vd = ln_stats(y_bufs[t2i].ap(), psA.ap()[:, 0:DM], par,
                                     [("v", y_mult_done)])
                ln4_sq_s = P.ctr["s"]
                xr = x_res.ap()[:, slot * DM:(slot + 1) * DM]
                vstt4 = P.emit("vector", lambda e, xr=xr, o=o, t2i=t2i:
                    e.scalar_tensor_tensor(
                        xr, y_bufs[t2i].ap(),
                        stats.ap()[:, C_RSTD + o:C_RSTD + o + 1], xr,
                        op0=ALU.mult, op1=ALU.add),
                    waits=[("v", vd), ("s", ss), ("v", ln3_v_done)], inc="v")
                vfin = P.emit("vector", lambda e, xr=xr, o=o: e.tensor_scalar(
                    xr, xr, stats.ap()[:, C_NMR + o:C_NMR + o + 1], None,
                    op0=ALU.add), waits=[("v", vstt4)], inc="v")
                ln4_v_done = vfin
                out_dx.append(P.emit("sync", lambda e, slot=slot: e.dma_start(
                    out=y_d[slot * 128:(slot + 1) * 128, :],
                    in_=x_res.ap()[:, slot * DM:(slot + 1) * DM]),
                    waits=[("v", vfin)], inc="dout"))
            y_mult_done = P.ctr["v"]
        fuse_read = P.ctr["t"]

    dx_final = P.ctr["dout"]
    P.emit("sync", lambda e, v=dx_final: e.wait_ge(sems["dout"], v))

    return _finish()


def _own_rows(r):
    return np.concatenate([np.arange((2 * i + r) * 128, (2 * i + r + 1) * 128)
                           for i in range(NOWN)])


def _rope_tables():
    inv = 1.0 / (10000.0 ** (np.arange(0, HD, 2, dtype=np.float64) / HD))
    ang = np.concatenate([inv, inv])[:, None] * np.arange(S, dtype=np.float64)[None, :]
    cosf = np.cos(ang)
    sinf = np.sin(ang)
    sin2 = sinf.copy()
    sin2[0:64] *= -1.0
    sin2 = np.roll(sin2, 64, axis=0)
    return cosf, sin2


def _masks(r):
    m = np.zeros((NOWN, 128, 256), np.float32)
    tk = np.arange(256)
    p = np.arange(128)
    for i in range(NOWN):
        q = 2 * i + r
        gtq = q * 128 + p
        gtk = 2 * i * 128 + tk
        m[i] = np.where(gtk[None, :] <= gtq[:, None], 0.0, -1e30)
    return np.ascontiguousarray(m.transpose(1, 0, 2).reshape(128, NOWN * 256))


def kernel(L0_attn_x, L0_attn_Wq, L0_attn_Wk, L0_attn_Wv, L0_attn_Wo,
           L0_ffn_w_gate, L0_ffn_w_up, L0_ffn_w_down,
           L0_ffn_s_gate, L0_ffn_s_up, L0_ffn_s_down):
    x = np.ascontiguousarray(np.asarray(L0_attn_x, dtype=np.float32))
    Wq = np.asarray(L0_attn_Wq, dtype=np.float32)
    Wk = np.asarray(L0_attn_Wk, dtype=np.float32)
    Wv = np.asarray(L0_attn_Wv, dtype=np.float32)
    Wo = np.asarray(L0_attn_Wo, dtype=np.float32)
    wg = np.asarray(L0_ffn_w_gate).astype(np.int8)
    wu = np.asarray(L0_ffn_w_up).astype(np.int8)
    wd = np.asarray(L0_ffn_w_down).astype(np.int8)
    sg = np.asarray(L0_ffn_s_gate, dtype=np.float32)
    su = np.asarray(L0_ffn_s_up, dtype=np.float32)
    sd = np.asarray(L0_ffn_s_down, dtype=np.float32)

    def qkv_tile(W):
        return np.ascontiguousarray(
            W.reshape(H, HD, 16, 128).transpose(0, 3, 2, 1))

    wq_t = qkv_tile(Wq)
    wk_t = np.ascontiguousarray(
        qkv_tile(Wk).reshape(8, 2, 128, 16, 128).transpose(0, 2, 1, 3, 4)
        .reshape(8, 128, 4096))
    wv_t = np.ascontiguousarray(
        qkv_tile(Wv).reshape(8, 2, 128, 16, 128).transpose(0, 2, 3, 1, 4)
        .reshape(8, 128, 4096))
    wo_t = np.ascontiguousarray(Wo.reshape(4, 512, 16, 128).transpose(0, 3, 2, 1))
    wg_t = wg.reshape(16, 128, 64, 128).transpose(2, 1, 0, 3).reshape(64, 128, 2048)
    wu_t = wu.reshape(16, 128, 64, 128).transpose(2, 1, 0, 3).reshape(64, 128, 2048)
    wgu_t = np.ascontiguousarray(np.concatenate([wg_t, wu_t], axis=2))
    wd_t = np.ascontiguousarray(
        wd.reshape(8, 8, 128, 4, 512).transpose(3, 0, 2, 1, 4)
        .reshape(4, 8, 128, 4096))
    sg_t = np.ascontiguousarray(sg.reshape(64, 128).T)
    su_t = np.ascontiguousarray(su.reshape(64, 128).T)
    sdb = np.ascontiguousarray(np.broadcast_to(sd[None, :], (128, DM)))

    cosf, sin2 = _rope_tables()
    cosk = np.ascontiguousarray(cosf).astype(BF)
    sink2 = np.ascontiguousarray(sin2).astype(BF)
    identb = np.eye(128, dtype=BF)
    identf = np.eye(128, dtype=np.float32)
    ones = np.ones((1, 128), np.float32)

    nc = build_program()

    rows_by_role = [_own_rows(0), _own_rows(1)]
    role_data = []
    for r in (0, 1):
        rows = rows_by_role[r]
        role_data.append({
            "cosq": np.ascontiguousarray(cosf[:, rows] * ATT_SCALE).astype(BF),
            "sinq2": np.ascontiguousarray(sin2[:, rows] * ATT_SCALE).astype(BF),
            "maskq": _masks(r),
        })

    in_maps = []
    for c in range(8):
        b, r = c // 2, c % 2
        rows = rows_by_role[r]
        in_maps.append({
            "x_full": x[b],
            "x_own": np.ascontiguousarray(x[b][rows]),
            "wq_t": wq_t, "wk_t": wk_t, "wv_t": wv_t, "wo_t": wo_t,
            "wgu_t": wgu_t, "wd_t": wd_t,
            "sg": sg_t, "su": su_t, "sdb": sdb,
            "cosk": cosk, "sink2": sink2,
            "cosq": role_data[r]["cosq"], "sinq2": role_data[r]["sinq2"],
            "maskq": role_data[r]["maskq"],
            "identb": identb, "identf": identf, "ones": ones,
            "negshift": np.full((128, 1), -8.0, np.float32),
        })

    res = run_bass_kernel_spmd(nc, in_maps, core_ids=list(range(8)))

    out = np.empty((NB, S, DM), np.float32)
    for c in range(8):
        b, r = c // 2, c % 2
        out[b][rows_by_role[r]] = np.asarray(res.results[c]["y"])
    return out
